# version 05
# baseline (speedup 1.0000x reference)
import math
import numpy as np

import concourse.bass as bass
import concourse.mybir as mybir
from concourse.bass_utils import run_bass_kernel_spmd
from concourse.alu_op_type import AluOpType as OP

AF = mybir.ActivationFunctionType
F32 = mybir.dt.float32
BF16 = mybir.dt.bfloat16
AX = mybir.AxisListType.X

T = 4096
D = 1024
H = 16
DH = 64
DHR = 32
E = 8
II = 4096
NT = 8
NCH = 32
TQ = 512
DC = D // 128

EXP_C_A = 5.0
EXP_C_B = 5.0
SCALE_A = 1.0 / math.sqrt(DH)
SCALE_B = 1.0 / math.sqrt(DH + DHR)

ARENA_BYTES = 184 * 1024
P0 = 0
P1 = 64 * 1024
P2 = 130 * 1024


class Ctr:

    def __init__(self, nc, name):
        self.sem = nc.alloc_semaphore(name)
        self.val = 0
        self.max_waited = 0

    def inc(self, ins, n=1):
        ins.then_inc(self.sem, n)
        self.val += n
        return self.val

    def wait(self, eng, v=None):
        if v is None:
            v = self.val
        if v <= 0:
            return
        self.max_waited = max(self.max_waited, v)
        eng.wait_ge(self.sem, v)


class Build:
    def __init__(self):
        nc = bass.Bass("TRN2", target_bir_lowering=False)
        self.nc = nc

        dt = nc.dram_tensor
        self.x_in = dt("L0_attn_a_x", [T, D], F32, kind="ExternalInput").ap()
        self.w_qkv = dt("L0_attn_a_W_qkv", [D, 3 * D], F32, kind="ExternalInput").ap()
        self.w_dq = dt("L0_attn_b_W_DQ", [D, 4096], F32, kind="ExternalInput").ap()
        self.w_uq = dt("L0_attn_b_W_UQ", [4096, D], F32, kind="ExternalInput").ap()
        self.w_dkv = dt("L0_attn_b_W_DKV", [D, 256], F32, kind="ExternalInput").ap()
        self.w_uk = dt("L0_attn_b_W_UK", [256, D], F32, kind="ExternalInput").ap()
        self.w_uv = dt("L0_attn_b_W_UV", [256, D], F32, kind="ExternalInput").ap()
        self.w_qr = dt("L0_attn_b_W_QR", [4096, 512], F32, kind="ExternalInput").ap()
        self.w_kr = dt("L0_attn_b_W_KR", [D, DHR], F32, kind="ExternalInput").ap()
        self.w_o = dt("L0_attn_b_W_O", [D, D], F32, kind="ExternalInput").ap()
        self.gate = dt("L0_ffn_gate", [D, E], F32, kind="ExternalInput").ap()
        self.w1 = dt("L0_ffn_experts_w1", [E, D, II], F32, kind="ExternalInput").ap()
        self.w2 = dt("L0_ffn_experts_w2", [E, II, D], F32, kind="ExternalInput").ap()
        self.w3 = dt("L0_ffn_experts_w3", [E, D, II], F32, kind="ExternalInput").ap()
        self.out = dt("out", [T, D], F32, kind="ExternalOutput").ap()

        self.xr_a = dt("xr_a", [T, D], F32, kind="Internal").ap()
        self.xr_b = dt("xr_b", [T, D], F32, kind="Internal").ap()
        self.xn_tm = dt("xn_tm", [T, D], BF16, kind="Internal").ap()
        self.qa_t = [dt(f"qa{t}", [D, TQ], BF16, kind="Internal").ap() for t in range(NT)]
        self.ka_t = [dt(f"ka{t}", [D, TQ], BF16, kind="Internal").ap() for t in range(NT)]
        self.wqkv_bf = dt("wqkv_bf", [D, 3 * D], BF16, kind="Internal").ap()
        self.w1bf = dt("w1bf", [E, D, II], BF16, kind="Internal").ap()
        self.w2bf = dt("w2bf", [E, II, D], BF16, kind="Internal").ap()
        self.w3bf = dt("w3bf", [E, D, II], BF16, kind="Internal").ap()
        self.wdq_bf = dt("wdq_bf", [D, 4096], BF16, kind="Internal").ap()
        self.wuq_bf = dt("wuq_bf", [4096, D], BF16, kind="Internal").ap()
        self.wdkv_bf = dt("wdkv_bf", [D, 256], BF16, kind="Internal").ap()
        self.wuk_bf = dt("wuk_bf", [256, D], BF16, kind="Internal").ap()
        self.wuv_bf = dt("wuv_bf", [256, D], BF16, kind="Internal").ap()
        self.wqr_bf = dt("wqr_bf", [4096, 512], BF16, kind="Internal").ap()
        self.wkr_bf = dt("wkr_bf", [D, DHR], BF16, kind="Internal").ap()
        self.wo_bf = dt("wo_bf", [D, D], BF16, kind="Internal").ap()
        self.cq_dram = dt("cq_dram", [4096, T], BF16, kind="Internal").ap()
        self.qcb_dram = dt("qcb_dram", [D, T], BF16, kind="Internal").ap()
        self.qrb_dram = dt("qrb_dram", [512, T], BF16, kind="Internal").ap()
        self.kcb_dram = dt("kcb_dram", [D, T], BF16, kind="Internal").ap()

        ident = np.eye(128, dtype=np.float32)
        self.ident_f = nc.inline_tensor(ident, name="ident_f").ap()
        rope = self._rope_table()
        self.rope_dram = nc.inline_tensor(rope, name="rope_tab").ap()

        al = nc.alloc_sbuf_tensor
        self.arena = al("arena", [128, ARENA_BYTES], mybir.dt.int8)
        self.consts = al("consts", [128, 16], F32)
        self.stats = al("stats", [128, 16], F32)
        self.tw_all = al("tw_all", [128, NCH * E], F32)
        self.rl = al("rl", [128, 1024], F32)
        self.ident_sb = al("ident_sb", [128, 128], F32)
        self.ones_row = al("ones_row", [1, 64], F32)
        self.gate_sb = al("gate_sb", [128, DC * E], BF16)
        self.r8 = al("r8", [128, 48], F32)

        self.psum = nc.alloc_psum_tensor("ps", [128, 4096], F32)

        self.qS = Ctr(nc, "qS")
        self.qG = Ctr(nc, "qG")
        self.qA = Ctr(nc, "qA")
        self.qV = Ctr(nc, "qV")
        self.qW = Ctr(nc, "qW")
        self.cT = Ctr(nc, "cT")
        self.cV = Ctr(nc, "cV")
        self.cA = Ctr(nc, "cA")
        self.cP = Ctr(nc, "cP")
        self.wdone = {}

    @staticmethod
    def _rope_table():
        t = np.arange(T, dtype=np.float32)
        inv = 1.0 / 10000.0 ** (np.arange(0, DHR, 2, dtype=np.float32) / DHR)
        freqs = t[:, None] * inv[None, :]
        emb = np.concatenate([freqs, freqs], -1)
        sc = (np.cos(emb) + np.sin(emb)).astype(np.float32)
        return np.tile(sc.T, (4, 1)).copy()

    def carve(self, off_bytes, rows, elems, dtype):
        esz = 2 if dtype == BF16 else 4
        assert off_bytes % 64 == 0
        assert off_bytes + elems * esz <= ARENA_BYTES, (off_bytes, elems * esz)
        h = self.arena.bitcast(dtype)
        return h[0:rows, off_bytes // esz: off_bytes // esz + elems]

    def pbank(self, i, banks=1):
        return self.psum[:, i * 512:(i + banks) * 512]


UPTO = "all"
EXP_NOWAIT_K = False


def kernel(**inputs):
    x = np.ascontiguousarray(inputs["L0_attn_a_x"], dtype=np.float32)
    n_cores = 8
    b = build_program(UPTO)
    for c in (b.qS, b.qG, b.qA, b.cT, b.cV, b.cA, b.cP):
        assert c.max_waited <= c.val, (c.sem, c.max_waited, c.val)
    wnames = ["L0_attn_a_W_qkv", "L0_attn_b_W_DQ", "L0_attn_b_W_UQ",
              "L0_attn_b_W_DKV", "L0_attn_b_W_UK", "L0_attn_b_W_UV",
              "L0_attn_b_W_QR", "L0_attn_b_W_KR", "L0_attn_b_W_O",
              "L0_ffn_gate", "L0_ffn_experts_w1", "L0_ffn_experts_w2",
              "L0_ffn_experts_w3"]
    weights = {k: np.ascontiguousarray(inputs[k], dtype=np.float32) for k in wnames}
    in_maps = [dict(weights, L0_attn_a_x=np.ascontiguousarray(x[c]))
               for c in range(n_cores)]
    res = run_bass_kernel_spmd(b.nc, in_maps, core_ids=list(range(n_cores)))
    return np.stack([res.results[c]["out"] for c in range(n_cores)], axis=0)


def build_program(upto="all"):
    b = Build()
    nc = b.nc
    stop_after = {
        "ln1": "ln1", "proj_a": "proj_a", "attn_a": "attn_a", "ffn1": "ffn1",
        "b2": "b2", "attn_b": "attn_b", "all": "ffn2",
    }[upto]

    def bar():
        emit_barrier(b, blk)

    with nc.Block() as blk:
        emit_init(b, blk)
        bar()
        emit_ln(b, blk, b.x_in, router=False)
        if stop_after == "ln1":
            bar()
            emit_finish(b, blk, b.x_in)
            return b
        bar()
        emit_proj_a(b, blk)
        if stop_after == "proj_a":
            bar()
            emit_finish(b, blk, b.x_in)
            return b
        bar()
        emit_attn_a(b, blk, x_old=b.x_in, x_new=b.xr_a)
        if stop_after == "attn_a":
            bar()
            emit_finish(b, blk, b.xr_a)
            return b
        bar()
        emit_ln(b, blk, b.xr_a, router=True)
        emit_ffn_prologue(b, blk)
        bar()
        emit_ffn(b, blk, x_old=b.xr_a, x_new=b.xr_b)
        if stop_after == "ffn1":
            bar()
            emit_finish(b, blk, b.xr_b)
            return b
        bar()
        emit_ln(b, blk, b.xr_b, router=False)
        bar()
        emit_b1(b, blk)
        bar()
        emit_b2(b, blk)
        if stop_after == "b2":
            bar()
            emit_finish(b, blk, b.xr_b)
            return b
        bar()
        emit_attn_b(b, blk, x_old=b.xr_b, x_new=b.xr_a)
        if stop_after == "attn_b":
            bar()
            emit_finish(b, blk, b.xr_a)
            return b
        bar()
        emit_ln(b, blk, b.xr_a, router=True)
        emit_ffn_prologue(b, blk)
        bar()
        emit_ffn(b, blk, x_old=b.xr_a, x_new=b.out)
        bar()
        emit_finish(b, blk, None)
    return b



def emit_ln(b, blk, x_src, router):
    nc = b.nc
    NXB = 4
    xt = b.carve(P2, 128, NXB * 1024, F32)
    sqd = b.carve(P2 + 16 * 1024, 128, 1024, F32)
    stg = b.carve(P2 + 20 * 1024, 128, 2 * 1024, BF16)
    xnfm = b.carve(P0, 128, DC * T, BF16)
    st = b.stats

    a0, v0, s0, qa0 = b.cA.val, b.cV.val, b.qS.val, b.qA.val

    @blk.scalar
    def _(sc):
        for m in range(NCH):
            p8 = (m % 2) * 8
            p = m % NXB
            b.qS.wait(sc, s0 + 16 * (m + 1))
            if m >= 2:
                b.cV.wait(sc, v0 + 7 * (m - 1))
            b.cA.inc(sc.activation(sqd, xt[:, p * 1024:(p + 1) * 1024], AF.Square,
                                   accum_out=st[:, p8 + 1:p8 + 2]))
            b.cV.wait(sc, v0 + 7 * m + 5)
            b.cA.inc(sc.activation(st[:, p8 + 4:p8 + 5], st[:, p8 + 3:p8 + 4], AF.Sqrt))
            b.cV.wait(sc, v0 + 7 * m + 7)
            if m >= 2:
                b.qA.wait(sc, qa0 + 16 * (m - 1))
            b.cA.inc(sc.activation(stg[:, (m % 2) * 1024:(m % 2) * 1024 + 1024],
                                   xt[:, p * 1024:(p + 1) * 1024], AF.Identity,
                                   bias=st[:, p8 + 6:p8 + 7], scale=st[:, p8 + 5:p8 + 6]))
            b.cA.wait(sc)
            b.qA.inc(sc.dma_start(out=b.xn_tm[m * 128:(m + 1) * 128, :],
                                  in_=stg[:, (m % 2) * 1024:(m % 2) * 1024 + 1024]), 16)

    @blk.vector
    def _(v):
        for m in range(NCH):
            p8 = (m % 2) * 8
            p = m % NXB
            c = lambda k: st[:, p8 + k:p8 + k + 1]
            b.cA.wait(v, a0 + 3 * m + 1)
            b.cV.inc(v.reduce_sum(c(0), xt[:, p * 1024:(p + 1) * 1024], axis=AX))
            b.cV.wait(v)
            b.cV.inc(v.tensor_scalar(c(2), c(0), -1.0 / D, None, OP.mult))
            b.cV.wait(v)
            b.cV.inc(v.tensor_tensor(c(3), c(2), c(2), OP.mult))
            b.cV.wait(v)
            b.cV.inc(v.scalar_tensor_tensor(c(3), c(1), 1.0 / D, c(3), OP.mult, OP.subtract))
            b.cV.wait(v)
            b.cV.inc(v.tensor_scalar(c(3), c(3), 1e-5, None, OP.add))
            b.cA.wait(v, a0 + 3 * m + 2)
            b.cV.inc(v.reciprocal(c(5), c(4)))
            b.cV.wait(v)
            b.cV.inc(v.tensor_tensor(c(6), c(2), c(5), OP.mult))

    @blk.sync
    def _(s):
        for m in range(NCH):
            p = m % NXB
            if m >= NXB:
                b.cA.wait(s, a0 + 3 * (m - NXB) + 3)
            b.qS.inc(s.dma_start(out=xt[:, p * 1024:(p + 1) * 1024],
                                 in_=x_src[m * 128:(m + 1) * 128, :]), 16)
        b.qA.wait(s, qa0 + 16 * NCH)
        for d in range(DC):
            b.qS.inc(s.dma_start_transpose(xnfm[:, d * T:(d + 1) * T],
                                           b.xn_tm[:, d * 128:(d + 1) * 128]), 16)

    if not router:
        return

    t0 = b.cT.val
    vr0 = b.cV.val
    ar0 = b.cA.val
    sfm = b.qS.val

    @blk.tensor
    def _(t):
        b.qS.wait(t, sfm)
        for m in range(NCH):
            bank = 5 + (m % 2)
            if m >= 2:
                b.cV.wait(t, vr0 + 9 * (m - 2) + 3)
            for d in range(DC):
                ins = t.matmul(b.pbank(bank)[:, 0:E],
                               xnfm[:, d * T + m * 128: d * T + m * 128 + 128],
                               b.gate_sb[:, d * E:(d + 1) * E],
                               start=(d == 0), stop=(d == DC - 1))
            b.cT.inc(ins)

    @blk.vector
    def _(v):
        r = b.r8
        for m in range(NCH):
            bank = 5 + (m % 2)
            ps = b.pbank(bank)[:, 0:E]
            b.cT.wait(v, t0 + m + 1)
            b.cV.inc(v.reduce_max(r[:, 0:1], ps, axis=AX))
            b.cV.wait(v)
            b.cV.inc(v.tensor_scalar(r[:, 8:16], ps, r[:, 0:1], None, OP.is_ge))
            b.cV.wait(v)
            b.cV.inc(v.scalar_tensor_tensor(r[:, 16:24], r[:, 8:16], -3.0e38, ps,
                                            OP.mult, OP.add))
            b.cV.wait(v)
            b.cV.inc(v.reduce_max(r[:, 1:2], r[:, 16:24], axis=AX))
            b.cV.wait(v)
            b.cV.inc(v.tensor_scalar(r[:, 24:32], r[:, 16:24], r[:, 1:2], None, OP.is_ge))
            b.cV.wait(v)
            b.cV.inc(v.tensor_tensor(r[:, 2:3], r[:, 1:2], r[:, 0:1], OP.subtract))
            b.cA.wait(v, ar0 + m + 1)
            b.cV.inc(v.tensor_scalar(r[:, 4:5], r[:, 3:4], -1.0, 1.0, OP.mult, OP.add))
            b.cV.wait(v)
            b.cV.inc(v.tensor_scalar(r[:, 32:40], r[:, 8:16], r[:, 4:5], None, OP.mult))
            b.cV.wait(v)
            b.cV.inc(v.scalar_tensor_tensor(b.tw_all[:, m * 8:m * 8 + 8],
                                            r[:, 24:32], r[:, 3:4],
                                            r[:, 32:40], OP.mult, OP.add))

    @blk.scalar
    def _(sc):
        for m in range(NCH):
            b.cV.wait(sc, vr0 + 9 * m + 6)
            b.cA.inc(sc.activation(b.r8[:, 3:4], b.r8[:, 2:3], AF.Sigmoid))

def emit_barrier(b, blk):
    vals = [(c, c.val) for c in (b.qS, b.qG, b.qA, b.qV, b.cT, b.cV, b.cA, b.cP)]

    def waits(eng):
        for c, v in vals:
            if v:
                c.wait(eng, v)

    blk.sync(waits)
    blk.gpsimd(waits)
    blk.scalar(waits)
    blk.vector(waits)
    blk.tensor(waits)


def emit_init(b, blk):
    nc = b.nc

    @blk.vector
    def _(v):
        v.memset(b.consts[:, 0:1], 0.0)
        v.memset(b.consts[:, 1:2], -EXP_C_A)
        v.memset(b.consts[:, 2:3], -EXP_C_B)
        v.memset(b.ones_row[0:1, :], 1.0)
        b.cV.inc(v.memset(b.consts[:, 3:4], 1.0))

    @blk.sync
    def _(s):
        b.qS.inc(s.dma_start(out=b.ident_sb[:], in_=b.ident_f[:]), 16)

    @blk.gpsimd
    def _(g):
        for d in range(DC):
            b.qG.inc(g.dma_start(out=b.gate_sb[:, d * E:(d + 1) * E],
                                 in_=b.gate[d * 128:(d + 1) * 128, :]), 16)
        casts = [("wqkv", b.wqkv_bf, b.w_qkv)]
        for e in range(E):
            casts.append((f"w1_{e}", b.w1bf[e], b.w1[e]))
            casts.append((f"w3_{e}", b.w3bf[e], b.w3[e]))
            casts.append((f"w2_{e}", b.w2bf[e], b.w2[e]))
        casts += [("wdq", b.wdq_bf, b.w_dq), ("wdkv", b.wdkv_bf, b.w_dkv),
                  ("wuk", b.wuk_bf, b.w_uk), ("wuv", b.wuv_bf, b.w_uv),
                  ("wkr", b.wkr_bf, b.w_kr), ("wuq", b.wuq_bf, b.w_uq),
                  ("wqr", b.wqr_bf, b.w_qr), ("wo", b.wo_bf, b.w_o)]
        for name, dst, src in casts:
            b.wdone[name] = b.qW.inc(g.dma_start(out=dst, in_=src), 16)


def emit_proj_a(b, blk):
    nc = b.nc
    xnfm = b.carve(P0, 128, DC * T, BF16)
    vtm = b.carve(P1, 128, NCH * H * 65, BF16)
    wsb = b.carve(P2, 128, DC * 1024, BF16)
    qst = b.carve(P2 + 16 * 1024, 128, 2 * 512, BF16)

    t0, a0, v0, s0, qa0 = b.cT.val, b.cA.val, b.cV.val, b.qS.val, b.qA.val
    zc = b.consts[:, 0:1]

    @blk.vector
    def _(v):
        b.cV.inc(v.memset(vtm, 1.0))

    def load_wsb(sc, pk):
        for d in range(DC):
            b.qA.inc(sc.dma_start(
                out=wsb[:, d * 1024:(d + 1) * 1024],
                in_=b.wqkv_bf[d * 128:(d + 1) * 128, pk * 1024:(pk + 1) * 1024]), 16)

    @blk.tensor
    def _(t):
        j = 0
        for pk in range(2):
            for qc in range(8):
                for tt in range(NT):
                    b.qA.wait(t, qa0 + 16 * 8 * (pk + 1))
                    if j >= 4:
                        b.cA.wait(t, a0 + (j - 3))
                    for d in range(DC):
                        ins = t.matmul(
                            b.pbank(j % 4),
                            wsb[:, d * 1024 + qc * 128: d * 1024 + qc * 128 + 128],
                            xnfm[:, d * T + tt * TQ: d * T + (tt + 1) * TQ],
                            start=(d == 0), stop=(d == DC - 1))
                    b.cT.inc(ins)
                    j += 1
        for m in range(NCH):
            for vc in range(2):
                b.qA.wait(t, qa0 + 16 * 8 * 3)
                if j >= 4:
                    b.cA.wait(t, a0 + (j - 3))
                for d in range(DC):
                    ins = t.matmul(
                        b.pbank(j % 4),
                        xnfm[:, d * T + m * 128: d * T + m * 128 + 128],
                        wsb[:, d * 1024 + vc * 512: d * 1024 + (vc + 1) * 512],
                        start=(d == 0), stop=(d == DC - 1))
                b.cT.inc(ins)
                j += 1

    @blk.scalar
    def _(sc):
        b.qW.wait(sc, b.wdone["wqkv"])
        load_wsb(sc, 0)
        j = 0
        for pk in range(2):
            for qc in range(8):
                for tt in range(NT):
                    b.cT.wait(sc, t0 + j + 1)
                    if j >= 2:
                        b.qS.wait(sc, s0 + 16 * (j - 1))
                    b.cA.inc(sc.activation(qst[:, (j % 2) * 512:(j % 2) * 512 + 512],
                                           b.pbank(j % 4), AF.Identity,
                                           bias=zc, scale=1.0))
                    j += 1
                    if j in (64, 128):
                        load_wsb(sc, j // 64)
        b.cV.wait(sc, v0 + 1)
        for m in range(NCH):
            for vc in range(2):
                b.cT.wait(sc, t0 + j + 1)
                for hh in range(8):
                    h = vc * 8 + hh
                    ins = sc.activation(
                        vtm[:, m * 1040 + h * 65: m * 1040 + h * 65 + 64],
                        b.pbank(j % 4)[:, hh * 64:(hh + 1) * 64],
                        AF.Identity, bias=zc, scale=1.0)
                b.cA.inc(ins)
                j += 1

    @blk.sync
    def _(s):
        j = 0
        for pk in range(2):
            dst_list = b.qa_t if pk == 0 else b.ka_t
            for qc in range(8):
                for tt in range(NT):
                    b.cA.wait(s, a0 + j + 1)
                    b.qS.inc(s.dma_start(
                        out=dst_list[tt][qc * 128:(qc + 1) * 128, :],
                        in_=qst[:, (j % 2) * 512:(j % 2) * 512 + 512]), 16)
                    j += 1


def emit_attn_a(b, blk, x_old, x_new):
    nc = b.nc
    vtm = b.carve(P1, 128, NCH * H * 65, BF16)
    kb = b.carve(P0, 128, T, BF16)
    qb = b.carve(P0 + 8 * 1024, 128, 2 * DC * TQ, BF16)
    pt = b.carve(P0 + 24 * 1024, 128, 4 * TQ, BF16)
    yfm = b.carve(P0 + 28 * 1024, 128, 2 * TQ, F32)
    ytm = b.carve(P0 + 34 * 1024, 128, 4 * 1024, F32)
    xold = b.carve(P2, 128, 2 * 4096, F32)

    t0, a0, v0, p0, s0 = b.cT.val, b.cA.val, b.cV.val, b.cP.val, b.qS.val
    ntk = lambda tt: 4 * (tt + 1)

    ct_b, ca_b, cv_b, cp_b = {}, {}, {}, {}
    ct, ca, cv, cp = t0, a0, v0, p0
    for tt in range(NT):
        for h in range(H):
            ct_b[(tt, h)] = ct
            ca_b[(tt, h)] = ca
            cv_b[(tt, h)] = cv
            cp_b[(tt, h)] = cp
            ct += 2 * ntk(tt) + 2
            ca += ntk(tt) // 2
            cv += 4
            cp += 4
        cv += 1
        ct_b[(tt, H)] = ct
        cv_b[(tt, H)] = cv
    ct_end, ca_end, cv_end, cp_end = ct, ca, cv, cp

    def sidx(tt, h, c):
        return ct_b[(tt, h)] + (c + 1 if c < 2 else 4 * (c // 2) - 1 + (c % 2))

    qs_b = {}
    qs = b.qS.val
    for tt in range(NT):
        qs_b[("qb", tt)] = qs
        qs += 16
        qs_b[("xold", tt)] = qs
        qs += 16
        for h in range(H):
            qs += 16 * (tt + 1)
            qs_b[("k", tt, h)] = qs
        qs_b[("store", tt)] = qs + 16
        qs += 16
    qs_end = qs

    @blk.tensor
    def _(t):
        for tt in range(NT):
            n = ntk(tt)
            for h in range(H):
                b.qS.wait(t, qs_b[("k", tt, h)])
                b.qS.wait(t, qs_b[("qb", tt)] + 16)
                if h >= 2:
                    b.cV.wait(t, cv_b[(tt, h - 2)] + 3)

                def emit_S(c):
                    if c % 2 == 0:
                        b.cA.wait(t, ca_b[(tt, h)] + c // 2 - 1)
                    base = ((c // 2) % 2) * 3072 + (c % 2) * 512
                    b.cT.inc(t.matmul(
                        b.psum[:, base:base + 512],
                        kb[(h % 2) * 64:(h % 2) * 64 + 64, c * 128:(c + 1) * 128],
                        qb[(h % 2) * 64:(h % 2) * 64 + 64,
                           (tt % 2) * DC * TQ + (h // 2) * TQ:
                           (tt % 2) * DC * TQ + (h // 2) * TQ + TQ],
                        start=True, stop=True, skip_group_check=True))

                def emit_PV(c):
                    if c >= 4 * tt:
                        b.cP.wait(t, cp_b[(tt, h)] + (c - 4 * tt) + 1)
                    else:
                        b.cA.wait(t, ca_b[(tt, h)] + c // 2 + 1)
                    b.cT.inc(t.matmul(
                        b.pbank(2 + h % 2)[0:65, :],
                        vtm[:, c * 1040 + h * 65: c * 1040 + h * 65 + 65],
                        pt[:, (((c // 2) % 2) * 2 + (c % 2)) * TQ:
                           (((c // 2) % 2) * 2 + (c % 2)) * TQ + TQ],
                        start=(c == 0), stop=(c == n - 1), skip_group_check=True))

                emit_S(0)
                emit_S(1)
                emit_S(2)
                emit_S(3)
                for p in range(2, n // 2):
                    emit_PV(2 * p - 4)
                    emit_PV(2 * p - 3)
                    emit_S(2 * p)
                    emit_S(2 * p + 1)
                emit_PV(n - 4)
                emit_PV(n - 3)
                emit_PV(n - 2)
                emit_PV(n - 1)
                b.cV.wait(t, cv_b[(tt, h)] + 1)
                if h >= 2:
                    b.cV.wait(t, cv_b[(tt, h - 2)] + 4)
                b.cT.inc(t.matmul(
                    b.pbank(4 + h % 2)[0:64, :],
                    b.ones_row[0:1, 0:64],
                    b.rl[0:1, (h % 2) * TQ:(h % 2) * TQ + TQ],
                    start=True, stop=True, skip_group_check=True))
                b.cV.wait(t, cv_b[(tt, h)] + 3)
                for q in range(4):
                    ins = t.transpose(
                        b.pbank(4 + h % 2)[:, q * 64:(q + 1) * 64],
                        yfm[0:64, (h % 2) * TQ + q * 128:(h % 2) * TQ + (q + 1) * 128],
                        b.ident_sb[0:64, 0:64])
                b.cT.inc(ins)

    @blk.scalar
    def _(sc):
        for tt in range(NT):
            n = ntk(tt)
            for h in range(H):
                for p in range(n // 2):
                    b.cT.wait(sc, sidx(tt, h, 2 * p + 1))
                    b.cA.inc(sc.activation(
                        pt[:, (p % 2) * 2 * TQ:(p % 2) * 2 * TQ + 2 * TQ],
                        b.psum[:, (p % 2) * 3072:(p % 2) * 3072 + 1024], AF.Exp,
                        bias=b.consts[:, 1:2], scale=SCALE_A))

    @blk.gpsimd
    def _(g):
        for tt in range(NT):
            for h in range(H):
                n = ntk(tt)
                for dc_ in range(4):
                    c = 4 * tt + dc_
                    b.cA.wait(g, ca_b[(tt, h)] + c // 2 + 1)
                    sl = pt[:, (((c // 2) % 2) * 2 + (c % 2)) * TQ:
                            (((c // 2) % 2) * 2 + (c % 2)) * TQ + TQ]
                    b.cP.inc(g.affine_select(sl, sl, pattern=[[1, TQ]],
                                             compare_op=OP.is_ge, fill=0.0,
                                             base=tt * TQ - c * 128,
                                             channel_multiplier=-1))

    @blk.vector
    def _(v):
        for tt in range(NT):
            n = ntk(tt)
            for h in range(H):
                b.cT.wait(v, ct_b[(tt, h)] + 2 * n)
                if h >= 2:
                    b.cT.wait(v, ct_b[(tt, h - 2)] + 2 * ntk(tt) + 1)
                b.cV.inc(v.reciprocal(b.rl[0:1, (h % 2) * TQ:(h % 2) * TQ + TQ],
                                      b.pbank(2 + h % 2)[64:65, :]))
                b.cV.inc(v.tensor_copy(yfm[0:64, (h % 2) * TQ:(h % 2) * TQ + TQ],
                                       b.pbank(2 + h % 2)[0:64, :]))
                b.cV.wait(v)
                b.cT.wait(v, ct_b[(tt, h)] + 2 * n + 1)
                b.cV.inc(v.tensor_tensor(yfm[0:64, (h % 2) * TQ:(h % 2) * TQ + TQ],
                                         yfm[0:64, (h % 2) * TQ:(h % 2) * TQ + TQ],
                                         b.pbank(4 + h % 2)[0:64, :],
                                         OP.mult))
                b.cT.wait(v, ct_b[(tt, h)] + 2 * n + 2)
                for q in range(4):
                    ins = v.tensor_copy(ytm[:, q * 1024 + h * 64: q * 1024 + h * 64 + 64],
                                        b.pbank(4 + h % 2)[:, q * 64:(q + 1) * 64])
                b.cV.inc(ins)
            b.qS.wait(v, qs_b[("xold", tt)] + 16)
            b.cV.wait(v)
            for q in range(4):
                ins = v.tensor_tensor(
                    xold[:, (tt % 2) * 4096 + q * 1024:(tt % 2) * 4096 + (q + 1) * 1024],
                    xold[:, (tt % 2) * 4096 + q * 1024:(tt % 2) * 4096 + (q + 1) * 1024],
                    ytm[:, q * 1024:(q + 1) * 1024], OP.add)
            b.cV.inc(ins)

    @blk.sync
    def _(s):
        for tt in range(NT):
            if tt >= 2:
                b.cT.wait(s, ct_b[(tt - 1, 0)])
                b.qS.wait(s, qs_b[("store", tt - 2)])
            b.qS.inc(s.dma_start(
                out=qb[:, (tt % 2) * DC * TQ:(tt % 2 + 1) * DC * TQ]
                    .rearrange("p (f w) -> p f w", f=DC),
                in_=b.qa_t[tt].rearrange("(f p) w -> p f w", p=128)), 16)
            b.qS.inc(s.dma_start(
                out=xold[:, (tt % 2) * 4096:(tt % 2 + 1) * 4096]
                    .rearrange("p (q d) -> p q d", q=4),
                in_=x_old[tt * TQ:(tt + 1) * TQ, :].rearrange("(q p) d -> p q d", p=128)), 16)
            for h in range(H):
                if EXP_NOWAIT_K:
                    pass
                elif h >= 2 or tt >= 1:
                    hh, tt_ = (h - 2, tt) if h >= 2 else (H - 2 + h, tt - 1)
                    b.cT.wait(s, ct_b[(tt_, hh)] + 2 * ntk(tt_))
                for sblk in range(tt + 1):
                    b.qS.inc(s.dma_start(
                        out=kb[(h % 2) * 64:(h % 2) * 64 + 64, sblk * TQ:(sblk + 1) * TQ],
                        in_=b.ka_t[sblk][h * DH:(h + 1) * DH, :]), 16)
            b.cV.wait(s, cv_b[(tt, H)])
            b.qS.inc(s.dma_start(
                out=x_new[tt * TQ:(tt + 1) * TQ, :].rearrange("(q p) d -> p q d", p=128),
                in_=xold[:, (tt % 2) * 4096:(tt % 2 + 1) * 4096]
                    .rearrange("p (q d) -> p q d", q=4)), 16)

    assert b.cT.val == ct_end and b.cA.val == ca_end
    assert b.cV.val == cv_end and b.cP.val == cp_end
    assert b.qS.val == qs_end, (b.qS.val, qs_end)


def emit_b1(b, blk):
    nc = b.nc
    xnfm = b.carve(P0, 128, DC * T, BF16)
    vtm = b.carve(P1, 128, NCH * H * 65, BF16)
    krT = b.carve(130 * 1024, 32, T, BF16)
    LB = 138 * 1024
    ckvT = b.carve(LB, 128, 2 * T, BF16)
    wdkv_sb = b.carve(LB + 16 * 1024, 128, DC * 256, BF16)
    wuk_sb = b.carve(LB + 20 * 1024, 128, 2 * 1024, BF16)
    wuv_sb = b.carve(LB + 24 * 1024, 128, 2 * 1024, BF16)
    wkr_sb = b.carve(LB + 28 * 1024, 128, DC * DHR, BF16)
    rope_sb = b.carve(LB + 29 * 1024, 128, 1024, F32)
    wdq_sb = [b.carve(LB + 33 * 1024 + k * 2 * 1024, 128, DC * 128, BF16)
              for k in range(2)]
    cqst = [b.carve(LB + 37 * 1024 + k * 2 * 1024, 128, 1024, BF16)
            for k in range(2)]
    kst = [b.carve(LB + 41 * 1024 + k * 2 * 1024, 128, 1024, BF16)
           for k in range(2)]

    t0, a0, v0, s0, qa0, qg0 = (b.cT.val, b.cA.val, b.cV.val, b.qS.val,
                                b.qA.val, b.qG.val)
    zc = b.consts[:, 0:1]
    pp = lambda g: b.psum[:, (g % 2) * 1024:(g % 2) * 1024 + 1024]

    CKV0, KCB0, KR0, V0_, CQ0 = 0, 8, 40, 44, 76

    @blk.scalar
    def _(sc):
        b.qW.wait(sc, b.wdone["wkr"])
        b.qA.inc(sc.dma_start(out=wdkv_sb.rearrange("p (c f) -> p c f", c=DC),
                              in_=b.wdkv_bf.rearrange("(c p) f -> p c f", p=128)), 16)
        b.qA.inc(sc.dma_start(out=wuk_sb.rearrange("p (c f) -> p c f", c=2),
                              in_=b.wuk_bf.rearrange("(c p) f -> p c f", p=128)), 16)
        b.qA.inc(sc.dma_start(out=wuv_sb.rearrange("p (c f) -> p c f", c=2),
                              in_=b.wuv_bf.rearrange("(c p) f -> p c f", p=128)), 16)
        b.qA.inc(sc.dma_start(out=wkr_sb.rearrange("p (c f) -> p c f", c=DC),
                              in_=b.wkr_bf.rearrange("(c p) f -> p c f", p=128)), 16)
        b.qA.inc(sc.dma_start(out=wdq_sb[0].rearrange("p (c f) -> p c f", c=DC),
                              in_=b.wdq_bf[:, 0:128].rearrange("(c p) f -> p c f", p=128)), 16)
        for g in range(8):
            b.cT.wait(sc, t0 + CKV0 + g + 1)
            fb, tt = divmod(g, 4)
            b.cA.inc(sc.activation(ckvT[:, fb * T + tt * 1024: fb * T + tt * 1024 + 1024],
                                   pp(g), AF.Identity, bias=zc, scale=1.0))
        for g in range(32):
            b.cT.wait(sc, t0 + KCB0 + g + 1)
            if g >= 2:
                b.qS.wait(sc, s0 + 16 * (g - 1))
            b.cA.inc(sc.activation(kst[g % 2], pp(g), AF.Identity, bias=zc, scale=1.0))
        b.cV.wait(sc, v0 + 1)
        for g in range(32):
            b.cT.wait(sc, t0 + V0_ + g + 1)
            for h in range(H):
                ins = sc.activation(vtm[:, g * 1040 + h * 65: g * 1040 + h * 65 + 64],
                                    pp(g)[:, h * 64:(h + 1) * 64],
                                    AF.Identity, bias=zc, scale=1.0)
            b.cA.inc(ins)
        for g in range(128):
            fb, tt = divmod(g, 4)
            if tt == 0 and fb + 1 < 32:
                if fb >= 1:
                    b.cT.wait(sc, t0 + CQ0 + 4 * fb)
                b.qA.inc(sc.dma_start(
                    out=wdq_sb[(fb + 1) % 2].rearrange("p (c f) -> p c f", c=DC),
                    in_=b.wdq_bf[:, (fb + 1) * 128:(fb + 2) * 128]
                        .rearrange("(c p) f -> p c f", p=128)), 16)
            b.cT.wait(sc, t0 + CQ0 + g + 1)
            if g >= 2:
                b.qS.wait(sc, s0 + 16 * 32 + 16 * (g - 1))
            b.cA.inc(sc.activation(cqst[g % 2], pp(g), AF.Identity, bias=zc, scale=1.0))

    @blk.vector
    def _(v):
        b.cV.inc(v.memset(vtm, 1.0))
        for g in range(4):
            b.cT.wait(v, t0 + KR0 + g + 1)
            b.qG.wait(v, qg0 + 16 * (g + 1))
            b.cV.inc(v.tensor_tensor(krT[0:32, g * 1024:(g + 1) * 1024],
                                     b.psum[0:32, (g % 2) * 1024:(g % 2) * 1024 + 1024],
                                     rope_sb[0:32, :], OP.mult))

    @blk.gpsimd
    def _(g):
        for tt in range(4):
            if tt >= 1:
                b.cV.wait(g, v0 + 1 + tt)
            b.qG.inc(g.dma_start(out=rope_sb,
                                 in_=b.rope_dram[:, tt * 1024:(tt + 1) * 1024]), 16)

    @blk.tensor
    def _(t):
        b.qA.wait(t, qa0 + 16 * 4)
        g = 0
        for fb in range(2):
            for tt in range(4):
                if g >= 2:
                    b.cA.wait(t, a0 + (g - 1))
                for cb in range(DC):
                    for th in range(2):
                        ins = t.matmul(pp(g)[:, th * 512:(th + 1) * 512],
                                       wdkv_sb[:, cb * 256 + fb * 128: cb * 256 + fb * 128 + 128],
                                       xnfm[:, cb * T + tt * 1024 + th * 512:
                                            cb * T + tt * 1024 + (th + 1) * 512],
                                       start=(cb == 0), stop=(cb == DC - 1),
                                       skip_group_check=True)
                b.cT.inc(ins)
                g += 1
        b.cA.wait(t, a0 + 8)
        for g2 in range(32):
            fb, tt = divmod(g2, 4)
            if g2 >= 2:
                b.cA.wait(t, a0 + 8 + (g2 - 1))
            for cb in range(2):
                for th in range(2):
                    ins = t.matmul(pp(g2)[:, th * 512:(th + 1) * 512],
                                   wuk_sb[:, cb * 1024 + fb * 128: cb * 1024 + fb * 128 + 128],
                                   ckvT[:, cb * T + tt * 1024 + th * 512:
                                        cb * T + tt * 1024 + (th + 1) * 512],
                                   start=(cb == 0), stop=(cb == 1),
                                   skip_group_check=True)
            b.cT.inc(ins)
        b.cA.wait(t, a0 + 40)
        for g3 in range(4):
            if g3 >= 2:
                b.cV.wait(t, v0 + 1 + (g3 - 1))
            for cb in range(DC):
                for th in range(2):
                    ins = t.matmul(b.psum[0:32, (g3 % 2) * 1024 + th * 512:
                                          (g3 % 2) * 1024 + (th + 1) * 512],
                                   wkr_sb[:, cb * DHR:(cb + 1) * DHR],
                                   xnfm[:, cb * T + g3 * 1024 + th * 512:
                                        cb * T + g3 * 1024 + (th + 1) * 512],
                                   start=(cb == 0), stop=(cb == DC - 1),
                                   skip_group_check=True)
            b.cT.inc(ins)
        b.cV.wait(t, v0 + 5)
        for g4 in range(32):
            if g4 >= 2:
                b.cA.wait(t, a0 + 40 + (g4 - 1))
            for cb in range(2):
                for th in range(2):
                    ins = t.matmul(pp(g4)[:, th * 512:(th + 1) * 512],
                                   ckvT[:, cb * T + g4 * 128: cb * T + g4 * 128 + 128],
                                   wuv_sb[:, cb * 1024 + th * 512:
                                          cb * 1024 + (th + 1) * 512],
                                   start=(cb == 0), stop=(cb == 1),
                                   skip_group_check=True)
            b.cT.inc(ins)
        b.cA.wait(t, a0 + 72)
        for g5 in range(128):
            fb, tt = divmod(g5, 4)
            if tt == 0:
                b.qA.wait(t, qa0 + 16 * (5 + fb))
            if g5 >= 2:
                b.cA.wait(t, a0 + 72 + (g5 - 1))
            for cb in range(DC):
                for th in range(2):
                    ins = t.matmul(pp(g5)[:, th * 512:(th + 1) * 512],
                                   wdq_sb[fb % 2][:, cb * 128:(cb + 1) * 128],
                                   xnfm[:, cb * T + tt * 1024 + th * 512:
                                        cb * T + tt * 1024 + (th + 1) * 512],
                                   start=(cb == 0), stop=(cb == DC - 1),
                                   skip_group_check=True)
            b.cT.inc(ins)

    @blk.sync
    def _(s):
        for g in range(32):
            fb, tt = divmod(g, 4)
            b.cA.wait(s, a0 + 8 + g + 1)
            b.qS.inc(s.dma_start(
                out=b.kcb_dram[fb * 128:(fb + 1) * 128, tt * 1024:(tt + 1) * 1024],
                in_=kst[g % 2]), 16)
        for g in range(128):
            fb, tt = divmod(g, 4)
            b.cA.wait(s, a0 + 72 + g + 1)
            b.qS.inc(s.dma_start(
                out=b.cq_dram[fb * 128:(fb + 1) * 128, tt * 1024:(tt + 1) * 1024],
                in_=cqst[g % 2]), 16)


def emit_b2(b, blk):
    nc = b.nc
    cqt = b.carve(P0, 128, 32 * 1024, BF16)
    LB = 138 * 1024
    wch = [b.carve(LB + k * 8 * 1024, 128, 32 * 128, BF16) for k in range(2)]
    qst = [b.carve(LB + 16 * 1024 + k * 2 * 1024, 128, 1024, BF16) for k in range(2)]
    qrst = [b.carve(LB + 20 * 1024 + k * 2 * 1024, 128, 1024, BF16) for k in range(2)]
    rope_sb = b.carve(LB + 24 * 1024, 128, 1024, F32)

    t0, a0, v0, s0, qa0, qg0 = (b.cT.val, b.cA.val, b.cV.val, b.qS.val,
                                b.qA.val, b.qG.val)
    zc = b.consts[:, 0:1]
    pp = lambda g: b.psum[:, (g % 2) * 1024:(g % 2) * 1024 + 1024]
    NJ = 12

    def wsrc(g):
        j = g % NJ
        if j < 8:
            return b.wuq_bf[:, j * 128:(j + 1) * 128]
        return b.wqr_bf[:, (j - 8) * 128:(j - 8 + 1) * 128]

    def wload(sc, g):
        b.qA.inc(sc.dma_start(out=wch[g % 2].rearrange("p (c f) -> p c f", c=32),
                              in_=wsrc(g).rearrange("(c p) f -> p c f", p=128)), 16)

    def freed(x):
        tt, j = divmod(x, NJ)
        if j < 8:
            return ("A", a0 + tt * 8 + j + 1)
        return ("V", v0 + tt * 4 + (j - 8) + 1)

    cqt_done = lambda tt: s0 + 16 * (44 * tt + 32)
    store_done = lambda tt, j: s0 + 16 * (44 * tt + 32 + j + 1)

    @blk.scalar
    def _(sc):
        b.qW.wait(sc, b.wdone["wqr"])
        wload(sc, 0)
        for tt in range(4):
            for j in range(NJ):
                g = tt * NJ + j
                if g + 1 < 4 * NJ:
                    if g >= 1:
                        b.cT.wait(sc, t0 + g)
                    wload(sc, g + 1)
                if j < 8:
                    b.cT.wait(sc, t0 + g + 1)
                    if j >= 2:
                        b.qS.wait(sc, store_done(tt, j - 2))
                    elif tt >= 1:
                        b.qS.wait(sc, store_done(tt - 1, NJ - 1))
                    b.cA.inc(sc.activation(qst[g % 2], pp(g), AF.Identity,
                                           bias=zc, scale=1.0))

    @blk.vector
    def _(v):
        for tt in range(4):
            for j in range(8, NJ):
                g = tt * NJ + j
                b.cT.wait(v, t0 + g + 1)
                b.qG.wait(v, qg0 + 16 * (tt + 1))
                if j >= 10:
                    b.qS.wait(v, store_done(tt, j - 2))
                elif tt >= 1:
                    b.qS.wait(v, store_done(tt - 1, NJ - 1))
                b.cV.inc(v.tensor_tensor(qrst[g % 2], pp(g), rope_sb, OP.mult))

    @blk.gpsimd
    def _(g):
        for tt in range(4):
            if tt >= 1:
                b.cV.wait(g, v0 + 4 * tt)
            b.qG.inc(g.dma_start(out=rope_sb,
                                 in_=b.rope_dram[:, tt * 1024:(tt + 1) * 1024]), 16)

    @blk.tensor
    def _(t):
        for tt in range(4):
            for j in range(NJ):
                g = tt * NJ + j
                b.qA.wait(t, qa0 + 16 * (g + 1))
                b.qS.wait(t, cqt_done(tt))
                if g >= 2:
                    kind, val = freed(g - 2)
                    if kind == "A":
                        b.cA.wait(t, val)
                    else:
                        b.cV.wait(t, val)
                for cq in range(32):
                    for th in range(2):
                        ins = t.matmul(pp(g)[:, th * 512:(th + 1) * 512],
                                       wch[g % 2][:, cq * 128:(cq + 1) * 128],
                                       cqt[:, cq * 1024 + th * 512:
                                            cq * 1024 + (th + 1) * 512],
                                       start=(cq == 0), stop=(cq == 31),
                                       skip_group_check=True)
                b.cT.inc(ins)

    @blk.sync
    def _(s):
        for tt in range(4):
            if tt >= 1:
                b.cT.wait(s, t0 + tt * NJ)
            for cq in range(32):
                b.qS.inc(s.dma_start(
                    out=cqt[:, cq * 1024:(cq + 1) * 1024],
                    in_=b.cq_dram[cq * 128:(cq + 1) * 128, tt * 1024:(tt + 1) * 1024]), 16)
            for j in range(NJ):
                if j < 8:
                    b.cA.wait(s, a0 + tt * 8 + j + 1)
                    b.qS.inc(s.dma_start(
                        out=b.qcb_dram[j * 128:(j + 1) * 128, tt * 1024:(tt + 1) * 1024],
                        in_=qst[(tt * NJ + j) % 2]), 16)
                else:
                    b.cV.wait(s, v0 + 4 * tt + (j - 8) + 1)
                    b.qS.inc(s.dma_start(
                        out=b.qrb_dram[(j - 8) * 128:(j - 8 + 1) * 128,
                                       tt * 1024:(tt + 1) * 1024],
                        in_=qrst[(tt * NJ + j) % 2]), 16)


def emit_attn_b(b, blk, x_old, x_new):
    nc = b.nc
    vtm = b.carve(P1, 128, NCH * H * 65, BF16)
    krT = b.carve(130 * 1024, 32, T, BF16)
    kb = b.carve(P0, 128, T, BF16)
    qbB = b.carve(P0 + 8 * 1024, 128, DC * TQ, BF16)
    qrB = b.carve(P0 + 16 * 1024, 32, H * TQ, BF16)
    pt = b.carve(P0 + 32 * 1024, 128, 4 * TQ, BF16)
    yfm = b.carve(P0 + 36 * 1024, 128, 2 * TQ, F32)
    yT = b.carve(P0 + 40 * 1024, 128, DC * TQ, BF16)
    wo_sb = b.carve(P0 + 48 * 1024, 128, DC * 1024, BF16)
    xold = b.carve(138 * 1024, 128, 2 * 4096, F32)

    t0, a0, v0, p0, s0, qa0 = (b.cT.val, b.cA.val, b.cV.val, b.cP.val,
                               b.qS.val, b.qA.val)
    ntk = lambda tt: 4 * (tt + 1)

    ct_b, ca_b, cv_b, cp_b = {}, {}, {}, {}
    ct, ca, cv, cp = t0, a0, v0, p0
    for tt in range(NT):
        for h in range(H):
            ct_b[(tt, h)] = ct
            ca_b[(tt, h)] = ca
            cv_b[(tt, h)] = cv
            cp_b[(tt, h)] = cp
            ct += 2 * ntk(tt) + 1
            ca += ntk(tt) // 2
            cv += 4
            cp += 4
        ct += 4
        cv += 4
        ct_b[(tt, H)] = ct
        cv_b[(tt, H)] = cv
    ct_end, ca_end, cv_end, cp_end = ct, ca, cv, cp

    def sidx(tt, h, c):
        return ct_b[(tt, h)] + (c + 1 if c < 2 else 4 * (c // 2) - 1 + (c % 2))

    qs_b = {}
    qs = s0
    for tt in range(NT):
        qs_b[("qb", tt)] = qs + 16 * 2
        qs_b[("xold", tt)] = qs + 16 * 3
        for h in range(H):
            qs_b[("k", tt, h)] = qs + 16 * (3 + h + 1)
        qs_b[("store", tt)] = qs + 16 * 20
        qs += 16 * 20

    @blk.tensor
    def _(t):
        b.qA.wait(t, qa0 + 16)
        for tt in range(NT):
            n = ntk(tt)
            for h in range(H):
                b.qS.wait(t, qs_b[("k", tt, h)])
                b.qS.wait(t, qs_b[("qb", tt)])
                if h >= 2:
                    b.cV.wait(t, cv_b[(tt, h - 2)] + 3)
                if h == 0 and tt >= 1:
                    b.cV.wait(t, cv_b[(tt - 1, H)])

                def emit_S(c):
                    if c % 2 == 0:
                        b.cA.wait(t, ca_b[(tt, h)] + c // 2 - 1)
                    base = ((c // 2) % 2) * 3072 + (c % 2) * 512
                    t.matmul(
                        b.psum[:, base:base + 512],
                        kb[(h % 2) * 64:(h % 2) * 64 + 64, c * 128:(c + 1) * 128],
                        qbB[(h % 2) * 64:(h % 2) * 64 + 64,
                            (h // 2) * TQ:(h // 2) * TQ + TQ],
                        start=True, stop=False, skip_group_check=True)
                    b.cT.inc(t.matmul(
                        b.psum[:, base:base + 512],
                        krT[0:32, c * 128:(c + 1) * 128],
                        qrB[0:32, h * TQ:(h + 1) * TQ],
                        start=False, stop=True, skip_group_check=True))

                def emit_PV(c):
                    if c >= 4 * tt:
                        b.cP.wait(t, cp_b[(tt, h)] + (c - 4 * tt) + 1)
                    else:
                        b.cA.wait(t, ca_b[(tt, h)] + c // 2 + 1)
                    b.cT.inc(t.matmul(
                        b.pbank(2 + h % 2)[0:65, :],
                        vtm[:, c * 1040 + h * 65: c * 1040 + h * 65 + 65],
                        pt[:, (((c // 2) % 2) * 2 + (c % 2)) * TQ:
                           (((c // 2) % 2) * 2 + (c % 2)) * TQ + TQ],
                        start=(c == 0), stop=(c == n - 1), skip_group_check=True))

                emit_S(0)
                emit_S(1)
                emit_S(2)
                emit_S(3)
                for p in range(2, n // 2):
                    emit_PV(2 * p - 4)
                    emit_PV(2 * p - 3)
                    emit_S(2 * p)
                    emit_S(2 * p + 1)
                emit_PV(n - 4)
                emit_PV(n - 3)
                emit_PV(n - 2)
                emit_PV(n - 1)
                b.cV.wait(t, cv_b[(tt, h)] + 1)
                if h >= 2:
                    b.cV.wait(t, cv_b[(tt, h - 2)] + 3)
                b.cT.inc(t.matmul(
                    b.pbank(4 + h % 2)[0:64, :],
                    b.ones_row[0:1, 0:64],
                    b.rl[0:1, (h % 2) * TQ:(h % 2) * TQ + TQ],
                    start=True, stop=True, skip_group_check=True))
            b.cV.wait(t, cv_b[(tt, H - 1)] + 4)
            for ts in range(4):
                if ts >= 1:
                    b.cV.wait(t, cv_b[(tt, H)] - 4 + ts)
                elif tt >= 1:
                    b.cV.wait(t, cv_b[(tt - 1, H)])
                for db in range(DC):
                    for hh in range(2):
                        ins = t.matmul(
                            b.psum[:, 3072 + hh * 512:3072 + (hh + 1) * 512],
                            yT[:, db * TQ + ts * 128: db * TQ + ts * 128 + 128],
                            wo_sb[:, db * 1024 + hh * 512: db * 1024 + (hh + 1) * 512],
                            start=(db == 0), stop=(db == DC - 1),
                            skip_group_check=True)
                b.cT.inc(ins)

    @blk.scalar
    def _(sc):
        b.qW.wait(sc, b.wdone["wo"])
        b.qA.inc(sc.dma_start(out=wo_sb.rearrange("p (c f) -> p c f", c=DC),
                              in_=b.wo_bf.rearrange("(c p) f -> p c f", p=128)), 16)
        for tt in range(NT):
            n = ntk(tt)
            for h in range(H):
                for p in range(n // 2):
                    b.cT.wait(sc, sidx(tt, h, 2 * p + 1))
                    b.cA.inc(sc.activation(
                        pt[:, (p % 2) * 2 * TQ:(p % 2) * 2 * TQ + 2 * TQ],
                        b.psum[:, (p % 2) * 3072:(p % 2) * 3072 + 1024], AF.Exp,
                        bias=b.consts[:, 2:3], scale=SCALE_B))

    @blk.gpsimd
    def _(g):
        for tt in range(NT):
            n = ntk(tt)
            for h in range(H):
                for dc_ in range(4):
                    c = 4 * tt + dc_
                    b.cA.wait(g, ca_b[(tt, h)] + c // 2 + 1)
                    sl = pt[:, (((c // 2) % 2) * 2 + (c % 2)) * TQ:
                            (((c // 2) % 2) * 2 + (c % 2)) * TQ + TQ]
                    b.cP.inc(g.affine_select(sl, sl, pattern=[[1, TQ]],
                                             compare_op=OP.is_ge, fill=0.0,
                                             base=tt * TQ - c * 128,
                                             channel_multiplier=-1))

    @blk.vector
    def _(v):
        for tt in range(NT):
            n = ntk(tt)
            for h in range(H):
                b.cT.wait(v, ct_b[(tt, h)] + 2 * n)
                if h >= 2:
                    b.cT.wait(v, ct_b[(tt, h - 2)] + 2 * n + 1)
                b.cV.inc(v.reciprocal(b.rl[0:1, (h % 2) * TQ:(h % 2) * TQ + TQ],
                                      b.pbank(2 + h % 2)[64:65, :]))
                b.cV.inc(v.tensor_copy(yfm[0:64, (h % 2) * TQ:(h % 2) * TQ + TQ],
                                       b.pbank(2 + h % 2)[0:64, :]))
                b.cV.wait(v)
                b.cT.wait(v, ct_b[(tt, h)] + 2 * n + 1)
                b.cV.inc(v.tensor_tensor(yfm[0:64, (h % 2) * TQ:(h % 2) * TQ + TQ],
                                         yfm[0:64, (h % 2) * TQ:(h % 2) * TQ + TQ],
                                         b.pbank(4 + h % 2)[0:64, :],
                                         OP.mult))
                b.cV.wait(v)
                b.cV.inc(v.tensor_copy(
                    yT[(h % 2) * 64:(h % 2) * 64 + 64,
                       (h // 2) * TQ:(h // 2) * TQ + TQ],
                    yfm[0:64, (h % 2) * TQ:(h % 2) * TQ + TQ]))
            b.qS.wait(v, qs_b[("xold", tt)])
            for ts in range(4):
                b.cT.wait(v, ct_b[(tt, H)] - 4 + ts + 1)
                b.cV.inc(v.tensor_tensor(
                    xold[:, (tt % 2) * 4096 + ts * 1024:(tt % 2) * 4096 + (ts + 1) * 1024],
                    xold[:, (tt % 2) * 4096 + ts * 1024:(tt % 2) * 4096 + (ts + 1) * 1024],
                    b.psum[:, 3072:4096], OP.add))

    @blk.sync
    def _(s):
        for tt in range(NT):
            if tt >= 1:
                b.cT.wait(s, ct_b[(tt - 1, H)] - 4)
            b.qS.inc(s.dma_start(
                out=qbB.rearrange("p (f w) -> p f w", f=DC),
                in_=b.qcb_dram[:, tt * TQ:(tt + 1) * TQ]
                    .rearrange("(f p) w -> p f w", p=128)), 16)
            b.qS.inc(s.dma_start(
                out=qrB.rearrange("p (f w) -> p f w", f=H),
                in_=b.qrb_dram[:, tt * TQ:(tt + 1) * TQ]
                    .rearrange("(f p) w -> p f w", p=DHR)), 16)
            if tt >= 2:
                b.qS.wait(s, qs_b[("store", tt - 2)])
            b.qS.inc(s.dma_start(
                out=xold[:, (tt % 2) * 4096:(tt % 2 + 1) * 4096]
                    .rearrange("p (q d) -> p q d", q=4),
                in_=x_old[tt * TQ:(tt + 1) * TQ, :].rearrange("(q p) d -> p q d", p=128)), 16)
            for h in range(H):
                if h >= 2 or tt >= 1:
                    hh, tt_ = (h - 2, tt) if h >= 2 else (H - 2 + h, tt - 1)
                    b.cT.wait(s, ct_b[(tt_, hh)] + 2 * ntk(tt_))
                b.qS.inc(s.dma_start(
                    out=kb[(h % 2) * 64:(h % 2) * 64 + 64, 0:(tt + 1) * TQ],
                    in_=b.kcb_dram[h * DH:(h + 1) * DH, 0:(tt + 1) * TQ]), 16)
            b.cV.wait(s, cv_b[(tt, H)])
            b.qS.inc(s.dma_start(
                out=x_new[tt * TQ:(tt + 1) * TQ, :].rearrange("(q p) d -> p q d", p=128),
                in_=xold[:, (tt % 2) * 4096:(tt % 2 + 1) * 4096]
                    .rearrange("p (q d) -> p q d", q=4)), 16)

    assert b.cT.val == ct_end, (b.cT.val, ct_end)
    assert b.cA.val == ca_end, (b.cA.val, ca_end)
    assert b.cV.val == cv_end, (b.cV.val, cv_end)
    assert b.cP.val == cp_end, (b.cP.val, cp_end)
    assert b.qS.val == qs, (b.qS.val, qs)


IC = 512
TTF = 1024
NTF = T // TTF
NIC = II // IC


def emit_ffn(b, blk, x_old, x_new):
    nc = b.nc
    xnfm = b.carve(P0, 128, DC * T, BF16)
    acc = b.carve(64 * 1024, 128, 8 * 1024, F32)
    w13 = [b.carve(96 * 1024 + k * 8 * 1024, 128, DC * IC, BF16) for k in range(4)]
    w2c = [b.carve(128 * 1024 + k * 8 * 1024, 128, 4 * 1024, BF16) for k in range(2)]
    hid = [b.carve(144 * 1024 + k * 8 * 1024, 128, 4 * 1024, BF16) for k in range(2)]
    sg = [b.carve(160 * 1024 + k * 4 * 1024, 128, 1024, F32) for k in range(2)]

    t0, a0, v0, s0, qa0, qg0 = (b.cT.val, b.cA.val, b.cV.val, b.qS.val,
                                b.qA.val, b.qG.val)
    NGI = NTF * E * NIC

    def gi_of(tt, e, ic):
        return (tt * E + e) * NIC + ic

    def tt_of(gi):
        return gi // (E * NIC)

    ct_g, ct_u, ct_eo = {}, {}, {}
    cv_m, cv_e = {}, {}
    ct, cv = t0, v0
    for gi in range(NGI + 1):
        for ib in range(4):
            if gi < NGI:
                ct += 1; ct_g[(gi, ib)] = ct
                ct += 1; ct_u[(gi, ib)] = ct
                cv += 1; cv_m[(gi, ib)] = cv
            if gi >= 1:
                ct += 1; ct_eo[(gi - 1, 2 * ib)] = ct
                ct += 1; ct_eo[(gi - 1, 2 * ib + 1)] = ct
                cv += 1; cv_e[(gi - 1, 2 * ib)] = cv
                cv += 1; cv_e[(gi - 1, 2 * ib + 1)] = cv
    ca_s = lambda gi, ib: a0 + 4 * gi + ib + 1
    qs_ld = lambda tt, ts: s0 + 256 * tt + 16 * (ts + 1)

    def emit_eo(t, gj, ts):
        tt = tt_of(gj)
        par = gj % 2
        if ts == 0:
            b.qG.wait(t, qg0 + 16 * (gj + 1))
            b.cV.wait(t, cv_m[(gj, 3)])
        if ts >= 2:
            b.cV.wait(t, cv_e[(gj, ts - 2)])
        elif gj >= 1:
            b.cV.wait(t, cv_e[(gj - 1, ts + 6)])
        for ib in range(4):
            for hh in range(2):
                ins = t.matmul(
                    b.psum[:, 2048 + (ts % 2) * 1024 + hh * 512:
                           2048 + (ts % 2) * 1024 + (hh + 1) * 512],
                    hid[par][:, ib * 1024 + ts * 128: ib * 1024 + ts * 128 + 128],
                    w2c[par][:, ib * 1024 + hh * 512: ib * 1024 + (hh + 1) * 512],
                    start=(ib == 0), stop=(ib == 3),
                    skip_group_check=True)
        b.cT.inc(ins)

    @blk.scalar
    def _(sc):
        for gi in range(NGI):
            for ib in range(4):
                b.cT.wait(sc, ct_g[(gi, ib)])
                if gi * 4 + ib >= 2:
                    gj, jb = divmod(gi * 4 + ib - 2, 4)
                    b.cV.wait(sc, cv_m[(gj, jb)])
                b.cA.inc(sc.activation(sg[ib % 2], b.psum[:, 0:1024], AF.Silu))
                if ib == 0:
                    gn = gi + 1
                    if gn < NGI:
                        _, rem_n = divmod(gn, E * NIC)
                        e_n, _ = divmod(rem_n, NIC)
                        b.qW.wait(sc, b.wdone[f"w3_{e_n}"])
                        emit_w13_load(b, sc, gn)

    @blk.gpsimd
    def _(g):
        for gi in range(NGI):
            _, rem = divmod(gi, E * NIC)
            e, ic = divmod(rem, NIC)
            if gi >= 2:
                b.cT.wait(g, ct_eo[(gi - 2, 7)])
            b.qW.wait(g, b.wdone[f"w2_{e}"])
            b.qG.inc(g.dma_start(
                out=w2c[gi % 2].rearrange("p (i h) -> p i h", i=4),
                in_=b.w2bf[e][ic * IC:(ic + 1) * IC, :]
                    .rearrange("(i p) h -> p i h", p=128)), 16)

    @blk.tensor
    def _(t):
        for gi in range(NGI + 1):
            tt = tt_of(gi) if gi < NGI else NTF - 1
            par = gi % 2
            for ib in range(4):
                if gi < NGI:
                    if ib == 0:
                        b.qA.wait(t, qa0 + 32 * gi)
                    if gi * 4 + ib >= 1:
                        gj, jb = divmod(gi * 4 + ib - 1, 4)
                        b.cA.wait(t, ca_s(gj, jb))
                    for cb in range(DC):
                        for th in range(2):
                            ins = t.matmul(
                                b.psum[:, th * 512:(th + 1) * 512],
                                w13[par * 2][:, cb * IC + ib * 128: cb * IC + ib * 128 + 128],
                                xnfm[:, cb * T + tt * TTF + th * 512:
                                     cb * T + tt * TTF + (th + 1) * 512],
                                start=(cb == 0), stop=(cb == DC - 1),
                                skip_group_check=True)
                    b.cT.inc(ins)
                    if gi * 4 + ib >= 1:
                        gj, jb = divmod(gi * 4 + ib - 1, 4)
                        b.cV.wait(t, cv_m[(gj, jb)])
                    for cb in range(DC):
                        for th in range(2):
                            ins = t.matmul(
                                b.psum[:, 1024 + th * 512:1024 + (th + 1) * 512],
                                w13[par * 2 + 1][:, cb * IC + ib * 128: cb * IC + ib * 128 + 128],
                                xnfm[:, cb * T + tt * TTF + th * 512:
                                     cb * T + tt * TTF + (th + 1) * 512],
                                start=(cb == 0), stop=(cb == DC - 1),
                                skip_group_check=True)
                    b.cT.inc(ins)
                if gi >= 1:
                    emit_eo(t, gi - 1, 2 * ib)
                    emit_eo(t, gi - 1, 2 * ib + 1)

    @blk.vector
    def _(v):
        for gi in range(NGI + 1):
            par = gi % 2
            for ib in range(4):
                if gi < NGI:
                    b.cA.wait(v, ca_s(gi, ib))
                    b.cT.wait(v, ct_u[(gi, ib)])
                    b.cV.inc(v.tensor_tensor(hid[par][:, ib * 1024:(ib + 1) * 1024],
                                             sg[ib % 2], b.psum[:, 1024:2048],
                                             OP.mult))
                if gi >= 1:
                    gj = gi - 1
                    tt, rem = divmod(gj, E * NIC)
                    e, ic = divmod(rem, NIC)
                    for ts in (2 * ib, 2 * ib + 1):
                        b.cT.wait(v, ct_eo[(gj, ts)])
                        if e == 0 and ic == 0:
                            b.qS.wait(v, qs_ld(tt, ts))
                        pb = b.psum[:, 2048 + (ts % 2) * 1024:
                                    2048 + (ts % 2) * 1024 + 1024]
                        b.cV.inc(v.scalar_tensor_tensor(
                            acc[:, ts * 1024:(ts + 1) * 1024],
                            pb, b.tw_all[:, (tt * 8 + ts) * 8 + e: (tt * 8 + ts) * 8 + e + 1],
                            acc[:, ts * 1024:(ts + 1) * 1024],
                            OP.mult, OP.add))

    @blk.sync
    def _(s):
        for tt in range(NTF):
            if tt > 0:
                b.qS.wait(s, s0 + 256 * tt)
            for ts in range(8):
                b.qS.inc(s.dma_start(
                    out=acc[:, ts * 1024:(ts + 1) * 1024],
                    in_=x_old[(tt * 8 + ts) * 128:(tt * 8 + ts) * 128 + 128, :]), 16)
            gl = gi_of(tt, E - 1, NIC - 1)
            for ts in range(8):
                b.cV.wait(s, cv_e[(gl, ts)])
                b.qS.inc(s.dma_start(
                    out=x_new[(tt * 8 + ts) * 128:(tt * 8 + ts) * 128 + 128, :],
                    in_=acc[:, ts * 1024:(ts + 1) * 1024]), 16)

    assert b.cT.val == t0 + 16 * NGI
    assert b.cA.val == a0 + 4 * NGI
    assert b.cV.val == v0 + 12 * NGI
    assert b.qA.val == qa0 + 32 * (NGI - 1)
    assert b.qG.val == qg0 + 16 * NGI
    assert b.qS.val == s0 + 256 * NTF


def emit_w13_load(b, sc, gi):
    _, rem = divmod(gi, E * NIC)
    e, ic = divmod(rem, NIC)
    par = gi % 2
    w13_0 = b.carve(96 * 1024 + (par * 2) * 8 * 1024, 128, DC * IC, BF16)
    w13_1 = b.carve(96 * 1024 + (par * 2 + 1) * 8 * 1024, 128, DC * IC, BF16)
    b.qA.inc(sc.dma_start(
        out=w13_0.rearrange("p (c i) -> p c i", c=DC),
        in_=b.w1bf[e][:, ic * IC:(ic + 1) * IC]
            .rearrange("(c p) i -> p c i", p=128)), 16)
    b.qA.inc(sc.dma_start(
        out=w13_1.rearrange("p (c i) -> p c i", c=DC),
        in_=b.w3bf[e][:, ic * IC:(ic + 1) * IC]
            .rearrange("(c p) i -> p c i", p=128)), 16)


def emit_ffn_prologue(b, blk):
    @blk.scalar
    def _(sc):
        b.qW.wait(sc, b.wdone["w3_0"])
        emit_w13_load(b, sc, 0)


def emit_finish(b, blk, x_loc):
    if x_loc is not None:
        xt = b.carve(P2, 128, 2 * 1024, F32)

        @blk.sync
        def _(s):
            for m in range(NCH):
                buf = xt[:, (m % 2) * 1024:(m % 2) * 1024 + 1024]
                b.qS.inc(s.dma_start(out=buf, in_=x_loc[m * 128:(m + 1) * 128, :]), 16)
                b.qS.wait(s)
                b.qS.inc(s.dma_start(out=b.out[m * 128:(m + 1) * 128, :], in_=buf), 16)
                if m >= 1:
                    b.qS.wait(s, b.qS.val - 16)

    @blk.sync
    def _(s):
        b.qS.wait(s)
        b.qG.wait(s)
        b.qA.wait(s)

    @blk.gpsimd
    def _(g):
        b.qS.wait(g)
        b.qG.wait(g)


# version 06
# speedup vs baseline: 1.1960x; 1.1960x over previous
import math
import numpy as np

import concourse.bass as bass
import concourse.mybir as mybir
from concourse.bass_utils import run_bass_kernel_spmd
from concourse.alu_op_type import AluOpType as OP

AF = mybir.ActivationFunctionType
F32 = mybir.dt.float32
BF16 = mybir.dt.bfloat16
AX = mybir.AxisListType.X

T = 4096
D = 1024
H = 16
DH = 64
DHR = 32
E = 8
II = 4096
NT = 8
NCH = 32
TQ = 512
DC = D // 128

EXP_C_A = 5.0
EXP_C_B = 5.0
SCALE_A = 1.0 / math.sqrt(DH)
SCALE_B = 1.0 / math.sqrt(DH + DHR)

ARENA_BYTES = 184 * 1024
P0 = 0
P1 = 64 * 1024
P2 = 130 * 1024


class Ctr:

    def __init__(self, nc, name):
        self.sem = nc.alloc_semaphore(name)
        self.val = 0
        self.max_waited = 0

    def inc(self, ins, n=1):
        ins.then_inc(self.sem, n)
        self.val += n
        return self.val

    def wait(self, eng, v=None):
        if v is None:
            v = self.val
        if v <= 0:
            return
        self.max_waited = max(self.max_waited, v)
        eng.wait_ge(self.sem, v)


class Build:
    def __init__(self):
        nc = bass.Bass("TRN2", target_bir_lowering=False)
        self.nc = nc

        dt = nc.dram_tensor
        self.x_in = dt("L0_attn_a_x", [T, D], F32, kind="ExternalInput").ap()
        self.w_qkv = dt("L0_attn_a_W_qkv", [D, 3 * D], F32, kind="ExternalInput").ap()
        self.w_dq = dt("L0_attn_b_W_DQ", [D, 4096], F32, kind="ExternalInput").ap()
        self.w_uq = dt("L0_attn_b_W_UQ", [4096, D], F32, kind="ExternalInput").ap()
        self.w_dkv = dt("L0_attn_b_W_DKV", [D, 256], F32, kind="ExternalInput").ap()
        self.w_uk = dt("L0_attn_b_W_UK", [256, D], F32, kind="ExternalInput").ap()
        self.w_uv = dt("L0_attn_b_W_UV", [256, D], F32, kind="ExternalInput").ap()
        self.w_qr = dt("L0_attn_b_W_QR", [4096, 512], F32, kind="ExternalInput").ap()
        self.w_kr = dt("L0_attn_b_W_KR", [D, DHR], F32, kind="ExternalInput").ap()
        self.w_o = dt("L0_attn_b_W_O", [D, D], F32, kind="ExternalInput").ap()
        self.gate = dt("L0_ffn_gate", [D, E], F32, kind="ExternalInput").ap()
        self.w1 = dt("L0_ffn_experts_w1", [E, D, II], F32, kind="ExternalInput").ap()
        self.w2 = dt("L0_ffn_experts_w2", [E, II, D], F32, kind="ExternalInput").ap()
        self.w3 = dt("L0_ffn_experts_w3", [E, D, II], F32, kind="ExternalInput").ap()
        self.out = dt("out", [T, D], F32, kind="ExternalOutput").ap()

        self.xr_a = dt("xr_a", [T, D], F32, kind="Internal").ap()
        self.xr_b = dt("xr_b", [T, D], F32, kind="Internal").ap()
        self.xn_tm = dt("xn_tm", [T, D], BF16, kind="Internal").ap()
        self.qa_t = [dt(f"qa{t}", [D, TQ], BF16, kind="Internal").ap() for t in range(NT)]
        self.ka_t = [dt(f"ka{t}", [D, TQ], BF16, kind="Internal").ap() for t in range(NT)]
        self.wqkv_bf = dt("wqkv_bf", [D, 3 * D], BF16, kind="Internal").ap()
        self.w1bf = dt("w1bf", [E, D, II], BF16, kind="Internal").ap()
        self.w2bf = dt("w2bf", [E, II, D], BF16, kind="Internal").ap()
        self.w3bf = dt("w3bf", [E, D, II], BF16, kind="Internal").ap()
        self.wdq_bf = dt("wdq_bf", [D, 4096], BF16, kind="Internal").ap()
        self.wuq_bf = dt("wuq_bf", [4096, D], BF16, kind="Internal").ap()
        self.wdkv_bf = dt("wdkv_bf", [D, 256], BF16, kind="Internal").ap()
        self.wuk_bf = dt("wuk_bf", [256, D], BF16, kind="Internal").ap()
        self.wuv_bf = dt("wuv_bf", [256, D], BF16, kind="Internal").ap()
        self.wqr_bf = dt("wqr_bf", [4096, 512], BF16, kind="Internal").ap()
        self.wkr_bf = dt("wkr_bf", [D, DHR], BF16, kind="Internal").ap()
        self.wo_bf = dt("wo_bf", [D, D], BF16, kind="Internal").ap()
        self.cq_dram = dt("cq_dram", [4096, T], BF16, kind="Internal").ap()
        self.qcb_dram = dt("qcb_dram", [D, T], BF16, kind="Internal").ap()
        self.qrb_dram = dt("qrb_dram", [512, T], BF16, kind="Internal").ap()
        self.kcb_dram = dt("kcb_dram", [D, T], BF16, kind="Internal").ap()

        ident = np.eye(128, dtype=np.float32)
        self.ident_f = nc.inline_tensor(ident, name="ident_f").ap()
        rope = self._rope_table()
        self.rope_dram = nc.inline_tensor(rope, name="rope_tab").ap()

        al = nc.alloc_sbuf_tensor
        self.arena = al("arena", [128, ARENA_BYTES], mybir.dt.int8)
        self.consts = al("consts", [128, 16], F32)
        self.stats = al("stats", [128, 16], F32)
        self.tw_all = al("tw_all", [128, NCH * E], F32)
        self.rl = al("rl", [128, 1024], F32)
        self.ident_sb = al("ident_sb", [128, 128], F32)
        self.ones_row = al("ones_row", [1, 64], F32)
        self.gate_sb = al("gate_sb", [128, DC * E], BF16)
        self.r8 = al("r8", [128, 48], F32)

        self.psum = nc.alloc_psum_tensor("ps", [128, 4096], F32)

        self.qS = Ctr(nc, "qS")
        self.qG = Ctr(nc, "qG")
        self.qA = Ctr(nc, "qA")
        self.qV = Ctr(nc, "qV")
        self.qW = Ctr(nc, "qW")
        self.cT = Ctr(nc, "cT")
        self.cV = Ctr(nc, "cV")
        self.cA = Ctr(nc, "cA")
        self.cP = Ctr(nc, "cP")
        self.wdone = {}

    @staticmethod
    def _rope_table():
        t = np.arange(T, dtype=np.float32)
        inv = 1.0 / 10000.0 ** (np.arange(0, DHR, 2, dtype=np.float32) / DHR)
        freqs = t[:, None] * inv[None, :]
        emb = np.concatenate([freqs, freqs], -1)
        sc = (np.cos(emb) + np.sin(emb)).astype(np.float32)
        return np.tile(sc.T, (4, 1)).copy()

    def carve(self, off_bytes, rows, elems, dtype):
        esz = 2 if dtype == BF16 else 4
        assert off_bytes % 64 == 0
        assert off_bytes + elems * esz <= ARENA_BYTES, (off_bytes, elems * esz)
        h = self.arena.bitcast(dtype)
        return h[0:rows, off_bytes // esz: off_bytes // esz + elems]

    def pbank(self, i, banks=1):
        return self.psum[:, i * 512:(i + banks) * 512]


UPTO = "all"
EXP_NOWAIT_K = False


def kernel(**inputs):
    x = np.ascontiguousarray(inputs["L0_attn_a_x"], dtype=np.float32)
    n_cores = 8
    b = build_program(UPTO)
    for c in (b.qS, b.qG, b.qA, b.cT, b.cV, b.cA, b.cP):
        assert c.max_waited <= c.val, (c.sem, c.max_waited, c.val)
    wnames = ["L0_attn_a_W_qkv", "L0_attn_b_W_DQ", "L0_attn_b_W_UQ",
              "L0_attn_b_W_DKV", "L0_attn_b_W_UK", "L0_attn_b_W_UV",
              "L0_attn_b_W_QR", "L0_attn_b_W_KR", "L0_attn_b_W_O",
              "L0_ffn_gate", "L0_ffn_experts_w1", "L0_ffn_experts_w2",
              "L0_ffn_experts_w3"]
    weights = {k: np.ascontiguousarray(inputs[k], dtype=np.float32) for k in wnames}
    in_maps = [dict(weights, L0_attn_a_x=np.ascontiguousarray(x[c]))
               for c in range(n_cores)]
    res = run_bass_kernel_spmd(b.nc, in_maps, core_ids=list(range(n_cores)))
    return np.stack([res.results[c]["out"] for c in range(n_cores)], axis=0)


def build_program(upto="all"):
    b = Build()
    nc = b.nc
    stop_after = {
        "ln1": "ln1", "proj_a": "proj_a", "attn_a": "attn_a", "ffn1": "ffn1",
        "b2": "b2", "attn_b": "attn_b", "all": "ffn2",
    }[upto]

    def bar():
        emit_barrier(b, blk)

    with nc.Block() as blk:
        emit_init(b, blk)
        bar()
        emit_ln(b, blk, b.x_in, router=False)
        if stop_after == "ln1":
            bar()
            emit_finish(b, blk, b.x_in)
            return b
        bar()
        emit_proj_a(b, blk)
        if stop_after == "proj_a":
            bar()
            emit_finish(b, blk, b.x_in)
            return b
        bar()
        emit_attn_a(b, blk, x_old=b.x_in, x_new=b.xr_a)
        if stop_after == "attn_a":
            bar()
            emit_finish(b, blk, b.xr_a)
            return b
        bar()
        emit_ln(b, blk, b.xr_a, router=True)
        emit_ffn_prologue(b, blk)
        bar()
        emit_ffn(b, blk, x_old=b.xr_a, x_new=b.xr_b)
        if stop_after == "ffn1":
            bar()
            emit_finish(b, blk, b.xr_b)
            return b
        bar()
        emit_ln(b, blk, b.xr_b, router=False)
        bar()
        emit_b1(b, blk)
        bar()
        emit_b2(b, blk)
        if stop_after == "b2":
            bar()
            emit_finish(b, blk, b.xr_b)
            return b
        bar()
        emit_attn_b(b, blk, x_old=b.xr_b, x_new=b.xr_a)
        if stop_after == "attn_b":
            bar()
            emit_finish(b, blk, b.xr_a)
            return b
        bar()
        emit_ln(b, blk, b.xr_a, router=True)
        emit_ffn_prologue(b, blk)
        bar()
        emit_ffn(b, blk, x_old=b.xr_a, x_new=b.out)
        bar()
        emit_finish(b, blk, None)
    return b



def emit_ln(b, blk, x_src, router):
    nc = b.nc
    NXB = 4
    xt = b.carve(P2, 128, NXB * 1024, F32)
    sqd = b.carve(P2 + 16 * 1024, 128, 1024, F32)
    stg = b.carve(P2 + 20 * 1024, 128, 2 * 1024, BF16)
    xnfm = b.carve(P0, 128, DC * T, BF16)
    st = b.stats

    a0, v0, s0, qa0 = b.cA.val, b.cV.val, b.qS.val, b.qA.val

    @blk.scalar
    def _(sc):
        for m in range(NCH):
            p8 = (m % 2) * 8
            p = m % NXB
            b.qS.wait(sc, s0 + 16 * (m + 1))
            if m >= 2:
                b.cV.wait(sc, v0 + 7 * (m - 1))
            b.cA.inc(sc.activation(sqd, xt[:, p * 1024:(p + 1) * 1024], AF.Square,
                                   accum_out=st[:, p8 + 1:p8 + 2]))
            b.cV.wait(sc, v0 + 7 * m + 5)
            b.cA.inc(sc.activation(st[:, p8 + 4:p8 + 5], st[:, p8 + 3:p8 + 4], AF.Sqrt))
            b.cV.wait(sc, v0 + 7 * m + 7)
            if m >= 2:
                b.qA.wait(sc, qa0 + 16 * (m - 1))
            b.cA.inc(sc.activation(stg[:, (m % 2) * 1024:(m % 2) * 1024 + 1024],
                                   xt[:, p * 1024:(p + 1) * 1024], AF.Identity,
                                   bias=st[:, p8 + 6:p8 + 7], scale=st[:, p8 + 5:p8 + 6]))
            b.cA.wait(sc)
            b.qA.inc(sc.dma_start(out=b.xn_tm[m * 128:(m + 1) * 128, :],
                                  in_=stg[:, (m % 2) * 1024:(m % 2) * 1024 + 1024]), 16)

    @blk.vector
    def _(v):
        for m in range(NCH):
            p8 = (m % 2) * 8
            p = m % NXB
            c = lambda k: st[:, p8 + k:p8 + k + 1]
            b.cA.wait(v, a0 + 3 * m + 1)
            b.cV.inc(v.reduce_sum(c(0), xt[:, p * 1024:(p + 1) * 1024], axis=AX))
            b.cV.wait(v)
            b.cV.inc(v.tensor_scalar(c(2), c(0), -1.0 / D, None, OP.mult))
            b.cV.wait(v)
            b.cV.inc(v.tensor_tensor(c(3), c(2), c(2), OP.mult))
            b.cV.wait(v)
            b.cV.inc(v.scalar_tensor_tensor(c(3), c(1), 1.0 / D, c(3), OP.mult, OP.subtract))
            b.cV.wait(v)
            b.cV.inc(v.tensor_scalar(c(3), c(3), 1e-5, None, OP.add))
            b.cA.wait(v, a0 + 3 * m + 2)
            b.cV.inc(v.reciprocal(c(5), c(4)))
            b.cV.wait(v)
            b.cV.inc(v.tensor_tensor(c(6), c(2), c(5), OP.mult))

    @blk.sync
    def _(s):
        for m in range(NCH):
            p = m % NXB
            if m >= NXB:
                b.cA.wait(s, a0 + 3 * (m - NXB) + 3)
            b.qS.inc(s.dma_start(out=xt[:, p * 1024:(p + 1) * 1024],
                                 in_=x_src[m * 128:(m + 1) * 128, :]), 16)
        b.qA.wait(s, qa0 + 16 * NCH)
        for d in range(DC):
            b.qS.inc(s.dma_start_transpose(xnfm[:, d * T:(d + 1) * T],
                                           b.xn_tm[:, d * 128:(d + 1) * 128]), 16)

    if not router:
        return

    t0 = b.cT.val
    vr0 = b.cV.val
    ar0 = b.cA.val
    sfm = b.qS.val

    @blk.tensor
    def _(t):
        b.qS.wait(t, sfm)
        for m in range(NCH):
            bank = 5 + (m % 2)
            if m >= 2:
                b.cV.wait(t, vr0 + 9 * (m - 2) + 3)
            for d in range(DC):
                ins = t.matmul(b.pbank(bank)[:, 0:E],
                               xnfm[:, d * T + m * 128: d * T + m * 128 + 128],
                               b.gate_sb[:, d * E:(d + 1) * E],
                               start=(d == 0), stop=(d == DC - 1))
            b.cT.inc(ins)

    @blk.vector
    def _(v):
        r = b.r8
        for m in range(NCH):
            bank = 5 + (m % 2)
            ps = b.pbank(bank)[:, 0:E]
            b.cT.wait(v, t0 + m + 1)
            b.cV.inc(v.reduce_max(r[:, 0:1], ps, axis=AX))
            b.cV.wait(v)
            b.cV.inc(v.tensor_scalar(r[:, 8:16], ps, r[:, 0:1], None, OP.is_ge))
            b.cV.wait(v)
            b.cV.inc(v.scalar_tensor_tensor(r[:, 16:24], r[:, 8:16], -3.0e38, ps,
                                            OP.mult, OP.add))
            b.cV.wait(v)
            b.cV.inc(v.reduce_max(r[:, 1:2], r[:, 16:24], axis=AX))
            b.cV.wait(v)
            b.cV.inc(v.tensor_scalar(r[:, 24:32], r[:, 16:24], r[:, 1:2], None, OP.is_ge))
            b.cV.wait(v)
            b.cV.inc(v.tensor_tensor(r[:, 2:3], r[:, 1:2], r[:, 0:1], OP.subtract))
            b.cA.wait(v, ar0 + m + 1)
            b.cV.inc(v.tensor_scalar(r[:, 4:5], r[:, 3:4], -1.0, 1.0, OP.mult, OP.add))
            b.cV.wait(v)
            b.cV.inc(v.tensor_scalar(r[:, 32:40], r[:, 8:16], r[:, 4:5], None, OP.mult))
            b.cV.wait(v)
            b.cV.inc(v.scalar_tensor_tensor(b.tw_all[:, m * 8:m * 8 + 8],
                                            r[:, 24:32], r[:, 3:4],
                                            r[:, 32:40], OP.mult, OP.add))

    @blk.scalar
    def _(sc):
        for m in range(NCH):
            b.cV.wait(sc, vr0 + 9 * m + 6)
            b.cA.inc(sc.activation(b.r8[:, 3:4], b.r8[:, 2:3], AF.Sigmoid))

def emit_barrier(b, blk):
    vals = [(c, c.val) for c in (b.qS, b.qG, b.qA, b.qV, b.cT, b.cV, b.cA, b.cP)]

    def waits(eng):
        for c, v in vals:
            if v:
                c.wait(eng, v)

    blk.sync(waits)
    blk.gpsimd(waits)
    blk.scalar(waits)
    blk.vector(waits)
    blk.tensor(waits)


def emit_init(b, blk):
    nc = b.nc

    @blk.vector
    def _(v):
        v.memset(b.consts[:, 0:1], 0.0)
        v.memset(b.consts[:, 1:2], -EXP_C_A)
        v.memset(b.consts[:, 2:3], -EXP_C_B)
        v.memset(b.ones_row[0:1, :], 1.0)
        b.cV.inc(v.memset(b.consts[:, 3:4], 1.0))

    @blk.sync
    def _(s):
        b.qS.inc(s.dma_start(out=b.ident_sb[:], in_=b.ident_f[:]), 16)

    @blk.gpsimd
    def _(g):
        for d in range(DC):
            b.qG.inc(g.dma_start(out=b.gate_sb[:, d * E:(d + 1) * E],
                                 in_=b.gate[d * 128:(d + 1) * 128, :]), 16)
        casts = [("wqkv", b.wqkv_bf, b.w_qkv)]
        for e in range(E):
            casts.append((f"w1_{e}", b.w1bf[e], b.w1[e]))
            casts.append((f"w3_{e}", b.w3bf[e], b.w3[e]))
            casts.append((f"w2_{e}", b.w2bf[e], b.w2[e]))
        casts += [("wdq", b.wdq_bf, b.w_dq), ("wdkv", b.wdkv_bf, b.w_dkv),
                  ("wuk", b.wuk_bf, b.w_uk), ("wuv", b.wuv_bf, b.w_uv),
                  ("wkr", b.wkr_bf, b.w_kr), ("wuq", b.wuq_bf, b.w_uq),
                  ("wqr", b.wqr_bf, b.w_qr), ("wo", b.wo_bf, b.w_o)]
        for name, dst, src in casts:
            b.wdone[name] = b.qW.inc(g.dma_start(out=dst, in_=src), 16)


def emit_proj_a(b, blk):
    nc = b.nc
    xnfm = b.carve(P0, 128, DC * T, BF16)
    vtm = b.carve(P1, 128, NCH * H * 65, BF16)
    wsb = b.carve(P2, 128, DC * 1024, BF16)
    qst = b.carve(P2 + 16 * 1024, 128, 2 * 512, BF16)

    t0, a0, v0, s0, qa0 = b.cT.val, b.cA.val, b.cV.val, b.qS.val, b.qA.val
    zc = b.consts[:, 0:1]

    @blk.vector
    def _(v):
        b.cV.inc(v.memset(vtm, 1.0))

    def load_wsb(sc, pk):
        for d in range(DC):
            b.qA.inc(sc.dma_start(
                out=wsb[:, d * 1024:(d + 1) * 1024],
                in_=b.wqkv_bf[d * 128:(d + 1) * 128, pk * 1024:(pk + 1) * 1024]), 16)

    @blk.tensor
    def _(t):
        j = 0
        for pk in range(2):
            for qc in range(8):
                for tt in range(NT):
                    b.qA.wait(t, qa0 + 16 * 8 * (pk + 1))
                    if j >= 4:
                        b.cA.wait(t, a0 + (j - 3))
                    for d in range(DC):
                        ins = t.matmul(
                            b.pbank(j % 4),
                            wsb[:, d * 1024 + qc * 128: d * 1024 + qc * 128 + 128],
                            xnfm[:, d * T + tt * TQ: d * T + (tt + 1) * TQ],
                            start=(d == 0), stop=(d == DC - 1))
                    b.cT.inc(ins)
                    j += 1
        for m in range(NCH):
            for vc in range(2):
                b.qA.wait(t, qa0 + 16 * 8 * 3)
                if j >= 4:
                    b.cA.wait(t, a0 + (j - 3))
                for d in range(DC):
                    ins = t.matmul(
                        b.pbank(j % 4),
                        xnfm[:, d * T + m * 128: d * T + m * 128 + 128],
                        wsb[:, d * 1024 + vc * 512: d * 1024 + (vc + 1) * 512],
                        start=(d == 0), stop=(d == DC - 1))
                b.cT.inc(ins)
                j += 1

    @blk.scalar
    def _(sc):
        b.qW.wait(sc, b.wdone["wqkv"])
        load_wsb(sc, 0)
        j = 0
        for pk in range(2):
            for qc in range(8):
                for tt in range(NT):
                    b.cT.wait(sc, t0 + j + 1)
                    if j >= 2:
                        b.qS.wait(sc, s0 + 16 * (j - 1))
                    b.cA.inc(sc.activation(qst[:, (j % 2) * 512:(j % 2) * 512 + 512],
                                           b.pbank(j % 4), AF.Identity,
                                           bias=zc, scale=1.0))
                    j += 1
                    if j in (64, 128):
                        load_wsb(sc, j // 64)
        b.cV.wait(sc, v0 + 1)
        for m in range(NCH):
            for vc in range(2):
                b.cT.wait(sc, t0 + j + 1)
                for hh in range(8):
                    h = vc * 8 + hh
                    ins = sc.activation(
                        vtm[:, m * 1040 + h * 65: m * 1040 + h * 65 + 64],
                        b.pbank(j % 4)[:, hh * 64:(hh + 1) * 64],
                        AF.Identity, bias=zc, scale=1.0)
                b.cA.inc(ins)
                j += 1

    @blk.sync
    def _(s):
        j = 0
        for pk in range(2):
            dst_list = b.qa_t if pk == 0 else b.ka_t
            for qc in range(8):
                for tt in range(NT):
                    b.cA.wait(s, a0 + j + 1)
                    b.qS.inc(s.dma_start(
                        out=dst_list[tt][qc * 128:(qc + 1) * 128, :],
                        in_=qst[:, (j % 2) * 512:(j % 2) * 512 + 512]), 16)
                    j += 1


def emit_attn_a(b, blk, x_old, x_new):
    nc = b.nc
    vtm = b.carve(P1, 128, NCH * H * 65, BF16)
    kb = b.carve(P0, 128, T, BF16)
    qb = b.carve(P0 + 8 * 1024, 128, 2 * DC * TQ, BF16)
    pt = b.carve(P0 + 24 * 1024, 128, 4 * TQ, BF16)
    yfm = b.carve(P0 + 28 * 1024, 128, 2 * TQ, F32)
    ytm = b.carve(P0 + 34 * 1024, 128, 4 * 1024, F32)
    xold = b.carve(P2, 128, 2 * 4096, F32)

    t0, a0, v0, p0, s0 = b.cT.val, b.cA.val, b.cV.val, b.cP.val, b.qS.val
    ntk = lambda tt: 4 * (tt + 1)

    ct_b, ca_b, cv_b, cp_b = {}, {}, {}, {}
    ct, ca, cv, cp = t0, a0, v0, p0
    for tt in range(NT):
        for h in range(H):
            ct_b[(tt, h)] = ct
            ca_b[(tt, h)] = ca
            cv_b[(tt, h)] = cv
            cp_b[(tt, h)] = cp
            ct += 2 * ntk(tt) + 2
            ca += ntk(tt) // 2
            cv += 4
            cp += 4
        cv += 1
        ct_b[(tt, H)] = ct
        cv_b[(tt, H)] = cv
    ct_end, ca_end, cv_end, cp_end = ct, ca, cv, cp

    def sidx(tt, h, c):
        return ct_b[(tt, h)] + (c + 1 if c < 2 else 4 * (c // 2) - 1 + (c % 2))

    qs_b = {}
    qs = b.qS.val
    for tt in range(NT):
        qs_b[("qb", tt)] = qs
        qs += 16
        qs_b[("xold", tt)] = qs
        qs += 16
        for h in range(H):
            qs += 16 * (tt + 1)
            qs_b[("k", tt, h)] = qs
        qs_b[("store", tt)] = qs + 16
        qs += 16
    qs_end = qs

    @blk.tensor
    def _(t):
        for tt in range(NT):
            n = ntk(tt)
            for h in range(H):
                b.qS.wait(t, qs_b[("k", tt, h)])
                b.qS.wait(t, qs_b[("qb", tt)] + 16)
                if h >= 2:
                    b.cV.wait(t, cv_b[(tt, h - 2)] + 3)

                def emit_S(c):
                    if c % 2 == 0:
                        b.cA.wait(t, ca_b[(tt, h)] + c // 2 - 1)
                    base = ((c // 2) % 2) * 3072 + (c % 2) * 512
                    b.cT.inc(t.matmul(
                        b.psum[:, base:base + 512],
                        kb[(h % 2) * 64:(h % 2) * 64 + 64, c * 128:(c + 1) * 128],
                        qb[(h % 2) * 64:(h % 2) * 64 + 64,
                           (tt % 2) * DC * TQ + (h // 2) * TQ:
                           (tt % 2) * DC * TQ + (h // 2) * TQ + TQ],
                        start=True, stop=True, skip_group_check=True))

                def emit_PV(c):
                    if c >= 4 * tt:
                        b.cP.wait(t, cp_b[(tt, h)] + (c - 4 * tt) + 1)
                    else:
                        b.cA.wait(t, ca_b[(tt, h)] + c // 2 + 1)
                    b.cT.inc(t.matmul(
                        b.pbank(2 + h % 2)[0:65, :],
                        vtm[:, c * 1040 + h * 65: c * 1040 + h * 65 + 65],
                        pt[:, (((c // 2) % 2) * 2 + (c % 2)) * TQ:
                           (((c // 2) % 2) * 2 + (c % 2)) * TQ + TQ],
                        start=(c == 0), stop=(c == n - 1), skip_group_check=True))

                emit_S(0)
                emit_S(1)
                emit_S(2)
                emit_S(3)
                for p in range(2, n // 2):
                    emit_PV(2 * p - 4)
                    emit_PV(2 * p - 3)
                    emit_S(2 * p)
                    emit_S(2 * p + 1)
                emit_PV(n - 4)
                emit_PV(n - 3)
                emit_PV(n - 2)
                emit_PV(n - 1)
                b.cV.wait(t, cv_b[(tt, h)] + 1)
                if h >= 2:
                    b.cV.wait(t, cv_b[(tt, h - 2)] + 4)
                b.cT.inc(t.matmul(
                    b.pbank(4 + h % 2)[0:64, :],
                    b.ones_row[0:1, 0:64],
                    b.rl[0:1, (h % 2) * TQ:(h % 2) * TQ + TQ],
                    start=True, stop=True, skip_group_check=True))
                b.cV.wait(t, cv_b[(tt, h)] + 3)
                for q in range(4):
                    ins = t.transpose(
                        b.pbank(4 + h % 2)[:, q * 64:(q + 1) * 64],
                        yfm[0:64, (h % 2) * TQ + q * 128:(h % 2) * TQ + (q + 1) * 128],
                        b.ident_sb[0:64, 0:64])
                b.cT.inc(ins)

    @blk.scalar
    def _(sc):
        for tt in range(NT):
            n = ntk(tt)
            for h in range(H):
                for p in range(n // 2):
                    b.cT.wait(sc, sidx(tt, h, 2 * p + 1))
                    b.cA.inc(sc.activation(
                        pt[:, (p % 2) * 2 * TQ:(p % 2) * 2 * TQ + 2 * TQ],
                        b.psum[:, (p % 2) * 3072:(p % 2) * 3072 + 1024], AF.Exp,
                        bias=b.consts[:, 1:2], scale=SCALE_A))

    @blk.gpsimd
    def _(g):
        for tt in range(NT):
            for h in range(H):
                n = ntk(tt)
                for dc_ in range(4):
                    c = 4 * tt + dc_
                    b.cA.wait(g, ca_b[(tt, h)] + c // 2 + 1)
                    sl = pt[:, (((c // 2) % 2) * 2 + (c % 2)) * TQ:
                            (((c // 2) % 2) * 2 + (c % 2)) * TQ + TQ]
                    b.cP.inc(g.affine_select(sl, sl, pattern=[[1, TQ]],
                                             compare_op=OP.is_ge, fill=0.0,
                                             base=tt * TQ - c * 128,
                                             channel_multiplier=-1))

    @blk.vector
    def _(v):
        for tt in range(NT):
            n = ntk(tt)
            for h in range(H):
                b.cT.wait(v, ct_b[(tt, h)] + 2 * n)
                if h >= 2:
                    b.cT.wait(v, ct_b[(tt, h - 2)] + 2 * ntk(tt) + 1)
                b.cV.inc(v.reciprocal(b.rl[0:1, (h % 2) * TQ:(h % 2) * TQ + TQ],
                                      b.pbank(2 + h % 2)[64:65, :]))
                b.cV.inc(v.tensor_copy(yfm[0:64, (h % 2) * TQ:(h % 2) * TQ + TQ],
                                       b.pbank(2 + h % 2)[0:64, :]))
                b.cV.wait(v)
                b.cT.wait(v, ct_b[(tt, h)] + 2 * n + 1)
                b.cV.inc(v.tensor_tensor(yfm[0:64, (h % 2) * TQ:(h % 2) * TQ + TQ],
                                         yfm[0:64, (h % 2) * TQ:(h % 2) * TQ + TQ],
                                         b.pbank(4 + h % 2)[0:64, :],
                                         OP.mult))
                b.cT.wait(v, ct_b[(tt, h)] + 2 * n + 2)
                for q in range(4):
                    ins = v.tensor_copy(ytm[:, q * 1024 + h * 64: q * 1024 + h * 64 + 64],
                                        b.pbank(4 + h % 2)[:, q * 64:(q + 1) * 64])
                b.cV.inc(ins)
            b.qS.wait(v, qs_b[("xold", tt)] + 16)
            b.cV.wait(v)
            for q in range(4):
                ins = v.tensor_tensor(
                    xold[:, (tt % 2) * 4096 + q * 1024:(tt % 2) * 4096 + (q + 1) * 1024],
                    xold[:, (tt % 2) * 4096 + q * 1024:(tt % 2) * 4096 + (q + 1) * 1024],
                    ytm[:, q * 1024:(q + 1) * 1024], OP.add)
            b.cV.inc(ins)

    @blk.sync
    def _(s):
        for tt in range(NT):
            if tt >= 2:
                b.cT.wait(s, ct_b[(tt - 1, 0)])
                b.qS.wait(s, qs_b[("store", tt - 2)])
            b.qS.inc(s.dma_start(
                out=qb[:, (tt % 2) * DC * TQ:(tt % 2 + 1) * DC * TQ]
                    .rearrange("p (f w) -> p f w", f=DC),
                in_=b.qa_t[tt].rearrange("(f p) w -> p f w", p=128)), 16)
            b.qS.inc(s.dma_start(
                out=xold[:, (tt % 2) * 4096:(tt % 2 + 1) * 4096]
                    .rearrange("p (q d) -> p q d", q=4),
                in_=x_old[tt * TQ:(tt + 1) * TQ, :].rearrange("(q p) d -> p q d", p=128)), 16)
            for h in range(H):
                if EXP_NOWAIT_K:
                    pass
                elif h >= 2 or tt >= 1:
                    hh, tt_ = (h - 2, tt) if h >= 2 else (H - 2 + h, tt - 1)
                    b.cT.wait(s, ct_b[(tt_, hh)] + 2 * ntk(tt_))
                for sblk in range(tt + 1):
                    b.qS.inc(s.dma_start(
                        out=kb[(h % 2) * 64:(h % 2) * 64 + 64, sblk * TQ:(sblk + 1) * TQ],
                        in_=b.ka_t[sblk][h * DH:(h + 1) * DH, :]), 16)
            b.cV.wait(s, cv_b[(tt, H)])
            b.qS.inc(s.dma_start(
                out=x_new[tt * TQ:(tt + 1) * TQ, :].rearrange("(q p) d -> p q d", p=128),
                in_=xold[:, (tt % 2) * 4096:(tt % 2 + 1) * 4096]
                    .rearrange("p (q d) -> p q d", q=4)), 16)

    assert b.cT.val == ct_end and b.cA.val == ca_end
    assert b.cV.val == cv_end and b.cP.val == cp_end
    assert b.qS.val == qs_end, (b.qS.val, qs_end)


def emit_b1(b, blk):
    nc = b.nc
    xnfm = b.carve(P0, 128, DC * T, BF16)
    vtm = b.carve(P1, 128, NCH * H * 65, BF16)
    krT = b.carve(130 * 1024, 32, T, BF16)
    LB = 138 * 1024
    ckvT = b.carve(LB, 128, 2 * T, BF16)
    wdkv_sb = b.carve(LB + 16 * 1024, 128, DC * 256, BF16)
    wuk_sb = b.carve(LB + 20 * 1024, 128, 2 * 1024, BF16)
    wuv_sb = b.carve(LB + 24 * 1024, 128, 2 * 1024, BF16)
    wkr_sb = b.carve(LB + 28 * 1024, 128, DC * DHR, BF16)
    rope_sb = b.carve(LB + 29 * 1024, 128, 1024, F32)
    wdq_sb = [b.carve(LB + 33 * 1024 + k * 2 * 1024, 128, DC * 128, BF16)
              for k in range(2)]
    cqst = [b.carve(LB + 37 * 1024 + k * 2 * 1024, 128, 1024, BF16)
            for k in range(2)]
    kst = [b.carve(LB + 41 * 1024 + k * 2 * 1024, 128, 1024, BF16)
           for k in range(2)]

    t0, a0, v0, s0, qa0, qg0 = (b.cT.val, b.cA.val, b.cV.val, b.qS.val,
                                b.qA.val, b.qG.val)
    zc = b.consts[:, 0:1]
    pp = lambda g: b.psum[:, (g % 2) * 1024:(g % 2) * 1024 + 1024]

    CKV0, KCB0, KR0, V0_, CQ0 = 0, 8, 40, 44, 76

    @blk.scalar
    def _(sc):
        b.qW.wait(sc, b.wdone["wkr"])
        b.qA.inc(sc.dma_start(out=wdkv_sb.rearrange("p (c f) -> p c f", c=DC),
                              in_=b.wdkv_bf.rearrange("(c p) f -> p c f", p=128)), 16)
        b.qA.inc(sc.dma_start(out=wuk_sb.rearrange("p (c f) -> p c f", c=2),
                              in_=b.wuk_bf.rearrange("(c p) f -> p c f", p=128)), 16)
        b.qA.inc(sc.dma_start(out=wuv_sb.rearrange("p (c f) -> p c f", c=2),
                              in_=b.wuv_bf.rearrange("(c p) f -> p c f", p=128)), 16)
        b.qA.inc(sc.dma_start(out=wkr_sb.rearrange("p (c f) -> p c f", c=DC),
                              in_=b.wkr_bf.rearrange("(c p) f -> p c f", p=128)), 16)
        b.qA.inc(sc.dma_start(out=wdq_sb[0].rearrange("p (c f) -> p c f", c=DC),
                              in_=b.wdq_bf[:, 0:128].rearrange("(c p) f -> p c f", p=128)), 16)
        for g in range(8):
            b.cT.wait(sc, t0 + CKV0 + g + 1)
            fb, tt = divmod(g, 4)
            b.cA.inc(sc.activation(ckvT[:, fb * T + tt * 1024: fb * T + tt * 1024 + 1024],
                                   pp(g), AF.Identity, bias=zc, scale=1.0))
        for g in range(32):
            b.cT.wait(sc, t0 + KCB0 + g + 1)
            if g >= 2:
                b.qS.wait(sc, s0 + 16 * (g - 1))
            b.cA.inc(sc.activation(kst[g % 2], pp(g), AF.Identity, bias=zc, scale=1.0))
        b.cV.wait(sc, v0 + 1)
        for g in range(32):
            b.cT.wait(sc, t0 + V0_ + g + 1)
            for h in range(H):
                ins = sc.activation(vtm[:, g * 1040 + h * 65: g * 1040 + h * 65 + 64],
                                    pp(g)[:, h * 64:(h + 1) * 64],
                                    AF.Identity, bias=zc, scale=1.0)
            b.cA.inc(ins)
        for g in range(128):
            fb, tt = divmod(g, 4)
            if tt == 0 and fb + 1 < 32:
                if fb >= 1:
                    b.cT.wait(sc, t0 + CQ0 + 4 * fb)
                b.qA.inc(sc.dma_start(
                    out=wdq_sb[(fb + 1) % 2].rearrange("p (c f) -> p c f", c=DC),
                    in_=b.wdq_bf[:, (fb + 1) * 128:(fb + 2) * 128]
                        .rearrange("(c p) f -> p c f", p=128)), 16)
            b.cT.wait(sc, t0 + CQ0 + g + 1)
            if g >= 2:
                b.qS.wait(sc, s0 + 16 * 32 + 16 * (g - 1))
            b.cA.inc(sc.activation(cqst[g % 2], pp(g), AF.Identity, bias=zc, scale=1.0))

    @blk.vector
    def _(v):
        b.cV.inc(v.memset(vtm, 1.0))
        for g in range(4):
            b.cT.wait(v, t0 + KR0 + g + 1)
            b.qG.wait(v, qg0 + 16 * (g + 1))
            b.cV.inc(v.tensor_tensor(krT[0:32, g * 1024:(g + 1) * 1024],
                                     b.psum[0:32, (g % 2) * 1024:(g % 2) * 1024 + 1024],
                                     rope_sb[0:32, :], OP.mult))

    @blk.gpsimd
    def _(g):
        for tt in range(4):
            if tt >= 1:
                b.cV.wait(g, v0 + 1 + tt)
            b.qG.inc(g.dma_start(out=rope_sb,
                                 in_=b.rope_dram[:, tt * 1024:(tt + 1) * 1024]), 16)

    @blk.tensor
    def _(t):
        b.qA.wait(t, qa0 + 16 * 4)
        g = 0
        for fb in range(2):
            for tt in range(4):
                if g >= 2:
                    b.cA.wait(t, a0 + (g - 1))
                for cb in range(DC):
                    for th in range(2):
                        ins = t.matmul(pp(g)[:, th * 512:(th + 1) * 512],
                                       wdkv_sb[:, cb * 256 + fb * 128: cb * 256 + fb * 128 + 128],
                                       xnfm[:, cb * T + tt * 1024 + th * 512:
                                            cb * T + tt * 1024 + (th + 1) * 512],
                                       start=(cb == 0), stop=(cb == DC - 1),
                                       skip_group_check=True)
                b.cT.inc(ins)
                g += 1
        b.cA.wait(t, a0 + 8)
        for g2 in range(32):
            fb, tt = divmod(g2, 4)
            if g2 >= 2:
                b.cA.wait(t, a0 + 8 + (g2 - 1))
            for cb in range(2):
                for th in range(2):
                    ins = t.matmul(pp(g2)[:, th * 512:(th + 1) * 512],
                                   wuk_sb[:, cb * 1024 + fb * 128: cb * 1024 + fb * 128 + 128],
                                   ckvT[:, cb * T + tt * 1024 + th * 512:
                                        cb * T + tt * 1024 + (th + 1) * 512],
                                   start=(cb == 0), stop=(cb == 1),
                                   skip_group_check=True)
            b.cT.inc(ins)
        b.cA.wait(t, a0 + 40)
        for g3 in range(4):
            if g3 >= 2:
                b.cV.wait(t, v0 + 1 + (g3 - 1))
            for cb in range(DC):
                for th in range(2):
                    ins = t.matmul(b.psum[0:32, (g3 % 2) * 1024 + th * 512:
                                          (g3 % 2) * 1024 + (th + 1) * 512],
                                   wkr_sb[:, cb * DHR:(cb + 1) * DHR],
                                   xnfm[:, cb * T + g3 * 1024 + th * 512:
                                        cb * T + g3 * 1024 + (th + 1) * 512],
                                   start=(cb == 0), stop=(cb == DC - 1),
                                   skip_group_check=True)
            b.cT.inc(ins)
        b.cV.wait(t, v0 + 5)
        for g4 in range(32):
            if g4 >= 2:
                b.cA.wait(t, a0 + 40 + (g4 - 1))
            for cb in range(2):
                for th in range(2):
                    ins = t.matmul(pp(g4)[:, th * 512:(th + 1) * 512],
                                   ckvT[:, cb * T + g4 * 128: cb * T + g4 * 128 + 128],
                                   wuv_sb[:, cb * 1024 + th * 512:
                                          cb * 1024 + (th + 1) * 512],
                                   start=(cb == 0), stop=(cb == 1),
                                   skip_group_check=True)
            b.cT.inc(ins)
        b.cA.wait(t, a0 + 72)
        for g5 in range(128):
            fb, tt = divmod(g5, 4)
            if tt == 0:
                b.qA.wait(t, qa0 + 16 * (5 + fb))
            if g5 >= 2:
                b.cA.wait(t, a0 + 72 + (g5 - 1))
            for cb in range(DC):
                for th in range(2):
                    ins = t.matmul(pp(g5)[:, th * 512:(th + 1) * 512],
                                   wdq_sb[fb % 2][:, cb * 128:(cb + 1) * 128],
                                   xnfm[:, cb * T + tt * 1024 + th * 512:
                                        cb * T + tt * 1024 + (th + 1) * 512],
                                   start=(cb == 0), stop=(cb == DC - 1),
                                   skip_group_check=True)
            b.cT.inc(ins)

    @blk.sync
    def _(s):
        for g in range(32):
            fb, tt = divmod(g, 4)
            b.cA.wait(s, a0 + 8 + g + 1)
            b.qS.inc(s.dma_start(
                out=b.kcb_dram[fb * 128:(fb + 1) * 128, tt * 1024:(tt + 1) * 1024],
                in_=kst[g % 2]), 16)
        for g in range(128):
            fb, tt = divmod(g, 4)
            b.cA.wait(s, a0 + 72 + g + 1)
            b.qS.inc(s.dma_start(
                out=b.cq_dram[fb * 128:(fb + 1) * 128, tt * 1024:(tt + 1) * 1024],
                in_=cqst[g % 2]), 16)


def emit_b2(b, blk):
    nc = b.nc
    cqt = b.carve(P0, 128, 32 * 1024, BF16)
    LB = 138 * 1024
    wch = [b.carve(LB + k * 8 * 1024, 128, 32 * 128, BF16) for k in range(2)]
    qst = [b.carve(LB + 16 * 1024 + k * 2 * 1024, 128, 1024, BF16) for k in range(2)]
    qrst = [b.carve(LB + 20 * 1024 + k * 2 * 1024, 128, 1024, BF16) for k in range(2)]
    rope_sb = b.carve(LB + 24 * 1024, 128, 1024, F32)

    t0, a0, v0, s0, qa0, qg0 = (b.cT.val, b.cA.val, b.cV.val, b.qS.val,
                                b.qA.val, b.qG.val)
    zc = b.consts[:, 0:1]
    pp = lambda g: b.psum[:, (g % 2) * 1024:(g % 2) * 1024 + 1024]
    NJ = 12

    def wsrc(g):
        j = g % NJ
        if j < 8:
            return b.wuq_bf[:, j * 128:(j + 1) * 128]
        return b.wqr_bf[:, (j - 8) * 128:(j - 8 + 1) * 128]

    def wload(sc, g):
        b.qA.inc(sc.dma_start(out=wch[g % 2].rearrange("p (c f) -> p c f", c=32),
                              in_=wsrc(g).rearrange("(c p) f -> p c f", p=128)), 16)

    def freed(x):
        tt, j = divmod(x, NJ)
        if j < 8:
            return ("A", a0 + tt * 8 + j + 1)
        return ("V", v0 + tt * 4 + (j - 8) + 1)

    cqt_done = lambda tt: s0 + 16 * (44 * tt + 32)
    store_done = lambda tt, j: s0 + 16 * (44 * tt + 32 + j + 1)

    @blk.scalar
    def _(sc):
        b.qW.wait(sc, b.wdone["wqr"])
        wload(sc, 0)
        for tt in range(4):
            for j in range(NJ):
                g = tt * NJ + j
                if g + 1 < 4 * NJ:
                    if g >= 1:
                        b.cT.wait(sc, t0 + g)
                    wload(sc, g + 1)
                if j < 8:
                    b.cT.wait(sc, t0 + g + 1)
                    if j >= 2:
                        b.qS.wait(sc, store_done(tt, j - 2))
                    elif tt >= 1:
                        b.qS.wait(sc, store_done(tt - 1, NJ - 1))
                    b.cA.inc(sc.activation(qst[g % 2], pp(g), AF.Identity,
                                           bias=zc, scale=1.0))

    @blk.vector
    def _(v):
        for tt in range(4):
            for j in range(8, NJ):
                g = tt * NJ + j
                b.cT.wait(v, t0 + g + 1)
                b.qG.wait(v, qg0 + 16 * (tt + 1))
                if j >= 10:
                    b.qS.wait(v, store_done(tt, j - 2))
                elif tt >= 1:
                    b.qS.wait(v, store_done(tt - 1, NJ - 1))
                b.cV.inc(v.tensor_tensor(qrst[g % 2], pp(g), rope_sb, OP.mult))

    @blk.gpsimd
    def _(g):
        for tt in range(4):
            if tt >= 1:
                b.cV.wait(g, v0 + 4 * tt)
            b.qG.inc(g.dma_start(out=rope_sb,
                                 in_=b.rope_dram[:, tt * 1024:(tt + 1) * 1024]), 16)

    @blk.tensor
    def _(t):
        for tt in range(4):
            for j in range(NJ):
                g = tt * NJ + j
                b.qA.wait(t, qa0 + 16 * (g + 1))
                b.qS.wait(t, cqt_done(tt))
                if g >= 2:
                    kind, val = freed(g - 2)
                    if kind == "A":
                        b.cA.wait(t, val)
                    else:
                        b.cV.wait(t, val)
                for cq in range(32):
                    for th in range(2):
                        ins = t.matmul(pp(g)[:, th * 512:(th + 1) * 512],
                                       wch[g % 2][:, cq * 128:(cq + 1) * 128],
                                       cqt[:, cq * 1024 + th * 512:
                                            cq * 1024 + (th + 1) * 512],
                                       start=(cq == 0), stop=(cq == 31),
                                       skip_group_check=True)
                b.cT.inc(ins)

    @blk.sync
    def _(s):
        for tt in range(4):
            if tt >= 1:
                b.cT.wait(s, t0 + tt * NJ)
            for cq in range(32):
                b.qS.inc(s.dma_start(
                    out=cqt[:, cq * 1024:(cq + 1) * 1024],
                    in_=b.cq_dram[cq * 128:(cq + 1) * 128, tt * 1024:(tt + 1) * 1024]), 16)
            for j in range(NJ):
                if j < 8:
                    b.cA.wait(s, a0 + tt * 8 + j + 1)
                    b.qS.inc(s.dma_start(
                        out=b.qcb_dram[j * 128:(j + 1) * 128, tt * 1024:(tt + 1) * 1024],
                        in_=qst[(tt * NJ + j) % 2]), 16)
                else:
                    b.cV.wait(s, v0 + 4 * tt + (j - 8) + 1)
                    b.qS.inc(s.dma_start(
                        out=b.qrb_dram[(j - 8) * 128:(j - 8 + 1) * 128,
                                       tt * 1024:(tt + 1) * 1024],
                        in_=qrst[(tt * NJ + j) % 2]), 16)


def emit_attn_b(b, blk, x_old, x_new):
    nc = b.nc
    vtm = b.carve(P1, 128, NCH * H * 65, BF16)
    krT = b.carve(130 * 1024, 32, T, BF16)
    kb = b.carve(P0, 128, T, BF16)
    qbB = b.carve(P0 + 8 * 1024, 128, DC * TQ, BF16)
    qrB = b.carve(P0 + 16 * 1024, 32, H * TQ, BF16)
    pt = b.carve(P0 + 32 * 1024, 128, 4 * TQ, BF16)
    yfm = b.carve(P0 + 36 * 1024, 128, 2 * TQ, F32)
    yT = b.carve(P0 + 40 * 1024, 128, DC * TQ, BF16)
    wo_sb = b.carve(P0 + 48 * 1024, 128, DC * 1024, BF16)
    xold = b.carve(138 * 1024, 128, 2 * 4096, F32)

    t0, a0, v0, p0, s0, qa0 = (b.cT.val, b.cA.val, b.cV.val, b.cP.val,
                               b.qS.val, b.qA.val)
    ntk = lambda tt: 4 * (tt + 1)

    ct_b, ca_b, cv_b, cp_b = {}, {}, {}, {}
    ct, ca, cv, cp = t0, a0, v0, p0
    for tt in range(NT):
        for h in range(H):
            ct_b[(tt, h)] = ct
            ca_b[(tt, h)] = ca
            cv_b[(tt, h)] = cv
            cp_b[(tt, h)] = cp
            ct += 2 * ntk(tt) + 1
            ca += ntk(tt) // 2
            cv += 4
            cp += 4
        ct += 4
        cv += 4
        ct_b[(tt, H)] = ct
        cv_b[(tt, H)] = cv
    ct_end, ca_end, cv_end, cp_end = ct, ca, cv, cp

    def sidx(tt, h, c):
        return ct_b[(tt, h)] + (c + 1 if c < 2 else 4 * (c // 2) - 1 + (c % 2))

    qs_b = {}
    qs = s0
    for tt in range(NT):
        qs_b[("qb", tt)] = qs + 16 * 2
        qs_b[("xold", tt)] = qs + 16 * 3
        for h in range(H):
            qs_b[("k", tt, h)] = qs + 16 * (3 + h + 1)
        qs_b[("store", tt)] = qs + 16 * 20
        qs += 16 * 20

    @blk.tensor
    def _(t):
        b.qA.wait(t, qa0 + 16)
        for tt in range(NT):
            n = ntk(tt)
            for h in range(H):
                b.qS.wait(t, qs_b[("k", tt, h)])
                b.qS.wait(t, qs_b[("qb", tt)])
                if h >= 2:
                    b.cV.wait(t, cv_b[(tt, h - 2)] + 3)
                if h == 0 and tt >= 1:
                    b.cV.wait(t, cv_b[(tt - 1, H)])

                def emit_S(c):
                    if c % 2 == 0:
                        b.cA.wait(t, ca_b[(tt, h)] + c // 2 - 1)
                    base = ((c // 2) % 2) * 3072 + (c % 2) * 512
                    t.matmul(
                        b.psum[:, base:base + 512],
                        kb[(h % 2) * 64:(h % 2) * 64 + 64, c * 128:(c + 1) * 128],
                        qbB[(h % 2) * 64:(h % 2) * 64 + 64,
                            (h // 2) * TQ:(h // 2) * TQ + TQ],
                        start=True, stop=False, skip_group_check=True)
                    b.cT.inc(t.matmul(
                        b.psum[:, base:base + 512],
                        krT[0:32, c * 128:(c + 1) * 128],
                        qrB[0:32, h * TQ:(h + 1) * TQ],
                        start=False, stop=True, skip_group_check=True))

                def emit_PV(c):
                    if c >= 4 * tt:
                        b.cP.wait(t, cp_b[(tt, h)] + (c - 4 * tt) + 1)
                    else:
                        b.cA.wait(t, ca_b[(tt, h)] + c // 2 + 1)
                    b.cT.inc(t.matmul(
                        b.pbank(2 + h % 2)[0:65, :],
                        vtm[:, c * 1040 + h * 65: c * 1040 + h * 65 + 65],
                        pt[:, (((c // 2) % 2) * 2 + (c % 2)) * TQ:
                           (((c // 2) % 2) * 2 + (c % 2)) * TQ + TQ],
                        start=(c == 0), stop=(c == n - 1), skip_group_check=True))

                emit_S(0)
                emit_S(1)
                emit_S(2)
                emit_S(3)
                for p in range(2, n // 2):
                    emit_PV(2 * p - 4)
                    emit_PV(2 * p - 3)
                    emit_S(2 * p)
                    emit_S(2 * p + 1)
                emit_PV(n - 4)
                emit_PV(n - 3)
                emit_PV(n - 2)
                emit_PV(n - 1)
                b.cV.wait(t, cv_b[(tt, h)] + 1)
                if h >= 2:
                    b.cV.wait(t, cv_b[(tt, h - 2)] + 3)
                b.cT.inc(t.matmul(
                    b.pbank(4 + h % 2)[0:64, :],
                    b.ones_row[0:1, 0:64],
                    b.rl[0:1, (h % 2) * TQ:(h % 2) * TQ + TQ],
                    start=True, stop=True, skip_group_check=True))
            b.cV.wait(t, cv_b[(tt, H - 1)] + 4)
            for ts in range(4):
                if ts >= 1:
                    b.cV.wait(t, cv_b[(tt, H)] - 4 + ts)
                elif tt >= 1:
                    b.cV.wait(t, cv_b[(tt - 1, H)])
                for db in range(DC):
                    for hh in range(2):
                        ins = t.matmul(
                            b.psum[:, 3072 + hh * 512:3072 + (hh + 1) * 512],
                            yT[:, db * TQ + ts * 128: db * TQ + ts * 128 + 128],
                            wo_sb[:, db * 1024 + hh * 512: db * 1024 + (hh + 1) * 512],
                            start=(db == 0), stop=(db == DC - 1),
                            skip_group_check=True)
                b.cT.inc(ins)

    @blk.scalar
    def _(sc):
        b.qW.wait(sc, b.wdone["wo"])
        b.qA.inc(sc.dma_start(out=wo_sb.rearrange("p (c f) -> p c f", c=DC),
                              in_=b.wo_bf.rearrange("(c p) f -> p c f", p=128)), 16)
        for tt in range(NT):
            n = ntk(tt)
            for h in range(H):
                for p in range(n // 2):
                    b.cT.wait(sc, sidx(tt, h, 2 * p + 1))
                    b.cA.inc(sc.activation(
                        pt[:, (p % 2) * 2 * TQ:(p % 2) * 2 * TQ + 2 * TQ],
                        b.psum[:, (p % 2) * 3072:(p % 2) * 3072 + 1024], AF.Exp,
                        bias=b.consts[:, 2:3], scale=SCALE_B))

    @blk.gpsimd
    def _(g):
        for tt in range(NT):
            n = ntk(tt)
            for h in range(H):
                for dc_ in range(4):
                    c = 4 * tt + dc_
                    b.cA.wait(g, ca_b[(tt, h)] + c // 2 + 1)
                    sl = pt[:, (((c // 2) % 2) * 2 + (c % 2)) * TQ:
                            (((c // 2) % 2) * 2 + (c % 2)) * TQ + TQ]
                    b.cP.inc(g.affine_select(sl, sl, pattern=[[1, TQ]],
                                             compare_op=OP.is_ge, fill=0.0,
                                             base=tt * TQ - c * 128,
                                             channel_multiplier=-1))

    @blk.vector
    def _(v):
        for tt in range(NT):
            n = ntk(tt)
            for h in range(H):
                b.cT.wait(v, ct_b[(tt, h)] + 2 * n)
                if h >= 2:
                    b.cT.wait(v, ct_b[(tt, h - 2)] + 2 * n + 1)
                b.cV.inc(v.reciprocal(b.rl[0:1, (h % 2) * TQ:(h % 2) * TQ + TQ],
                                      b.pbank(2 + h % 2)[64:65, :]))
                b.cV.inc(v.tensor_copy(yfm[0:64, (h % 2) * TQ:(h % 2) * TQ + TQ],
                                       b.pbank(2 + h % 2)[0:64, :]))
                b.cV.wait(v)
                b.cT.wait(v, ct_b[(tt, h)] + 2 * n + 1)
                b.cV.inc(v.tensor_tensor(yfm[0:64, (h % 2) * TQ:(h % 2) * TQ + TQ],
                                         yfm[0:64, (h % 2) * TQ:(h % 2) * TQ + TQ],
                                         b.pbank(4 + h % 2)[0:64, :],
                                         OP.mult))
                b.cV.wait(v)
                b.cV.inc(v.tensor_copy(
                    yT[(h % 2) * 64:(h % 2) * 64 + 64,
                       (h // 2) * TQ:(h // 2) * TQ + TQ],
                    yfm[0:64, (h % 2) * TQ:(h % 2) * TQ + TQ]))
            b.qS.wait(v, qs_b[("xold", tt)])
            for ts in range(4):
                b.cT.wait(v, ct_b[(tt, H)] - 4 + ts + 1)
                b.cV.inc(v.tensor_tensor(
                    xold[:, (tt % 2) * 4096 + ts * 1024:(tt % 2) * 4096 + (ts + 1) * 1024],
                    xold[:, (tt % 2) * 4096 + ts * 1024:(tt % 2) * 4096 + (ts + 1) * 1024],
                    b.psum[:, 3072:4096], OP.add))

    @blk.sync
    def _(s):
        for tt in range(NT):
            if tt >= 1:
                b.cT.wait(s, ct_b[(tt - 1, H)] - 4)
            b.qS.inc(s.dma_start(
                out=qbB.rearrange("p (f w) -> p f w", f=DC),
                in_=b.qcb_dram[:, tt * TQ:(tt + 1) * TQ]
                    .rearrange("(f p) w -> p f w", p=128)), 16)
            b.qS.inc(s.dma_start(
                out=qrB.rearrange("p (f w) -> p f w", f=H),
                in_=b.qrb_dram[:, tt * TQ:(tt + 1) * TQ]
                    .rearrange("(f p) w -> p f w", p=DHR)), 16)
            if tt >= 2:
                b.qS.wait(s, qs_b[("store", tt - 2)])
            b.qS.inc(s.dma_start(
                out=xold[:, (tt % 2) * 4096:(tt % 2 + 1) * 4096]
                    .rearrange("p (q d) -> p q d", q=4),
                in_=x_old[tt * TQ:(tt + 1) * TQ, :].rearrange("(q p) d -> p q d", p=128)), 16)
            for h in range(H):
                if h >= 2 or tt >= 1:
                    hh, tt_ = (h - 2, tt) if h >= 2 else (H - 2 + h, tt - 1)
                    b.cT.wait(s, ct_b[(tt_, hh)] + 2 * ntk(tt_))
                b.qS.inc(s.dma_start(
                    out=kb[(h % 2) * 64:(h % 2) * 64 + 64, 0:(tt + 1) * TQ],
                    in_=b.kcb_dram[h * DH:(h + 1) * DH, 0:(tt + 1) * TQ]), 16)
            b.cV.wait(s, cv_b[(tt, H)])
            b.qS.inc(s.dma_start(
                out=x_new[tt * TQ:(tt + 1) * TQ, :].rearrange("(q p) d -> p q d", p=128),
                in_=xold[:, (tt % 2) * 4096:(tt % 2 + 1) * 4096]
                    .rearrange("p (q d) -> p q d", q=4)), 16)

    assert b.cT.val == ct_end, (b.cT.val, ct_end)
    assert b.cA.val == ca_end, (b.cA.val, ca_end)
    assert b.cV.val == cv_end, (b.cV.val, cv_end)
    assert b.cP.val == cp_end, (b.cP.val, cp_end)
    assert b.qS.val == qs, (b.qS.val, qs)


IC = 512
TTF = 1024
NTF = T // TTF
NIC = II // IC


def emit_ffn(b, blk, x_old, x_new):
    nc = b.nc
    xnfm = b.carve(P0, 128, DC * T, BF16)
    acc = b.carve(64 * 1024, 128, 8 * 1024, F32)
    w13 = [b.carve(96 * 1024 + k * 8 * 1024, 128, DC * IC, BF16) for k in range(4)]
    w2c = [b.carve(128 * 1024 + k * 8 * 1024, 128, 4 * 1024, BF16) for k in range(2)]
    hid = [b.carve(144 * 1024 + k * 8 * 1024, 128, 4 * 1024, BF16) for k in range(2)]
    sg = [b.carve(160 * 1024 + k * 4 * 1024, 128, 1024, F32) for k in range(2)]

    t0, a0, v0, s0, qa0, qg0 = (b.cT.val, b.cA.val, b.cV.val, b.qS.val,
                                b.qA.val, b.qG.val)
    NGI = NTF * E * NIC

    def gi_of(tt, e, ic):
        return (tt * E + e) * NIC + ic

    def tt_of(gi):
        return gi // (E * NIC)

    ct_g, ct_u, ct_eo = {}, {}, {}
    cv_m, cv_e = {}, {}
    ct, cv = t0, v0
    for gi in range(NGI + 1):
        for ib in range(4):
            if gi < NGI:
                ct += 1; ct_g[(gi, ib)] = ct
                ct += 1; ct_u[(gi, ib)] = ct
                cv += 1; cv_m[(gi, ib)] = cv
            if gi >= 1:
                ct += 1; ct_eo[(gi - 1, 2 * ib)] = ct
                ct += 1; ct_eo[(gi - 1, 2 * ib + 1)] = ct
                cv += 1; cv_e[(gi - 1, 2 * ib)] = cv
                cv += 1; cv_e[(gi - 1, 2 * ib + 1)] = cv
    ca_s = lambda gi, ib: a0 + 4 * gi + ib + 1
    qs_ld = lambda tt, ts: s0 + 256 * tt + 16 * (ts + 1)

    def emit_eo(t, gj, ts):
        tt = tt_of(gj)
        par = gj % 2
        if ts == 0:
            b.qA.wait(t, qa0 + 48 * gj)
            b.cV.wait(t, cv_m[(gj, 3)])
        if ts >= 2:
            b.cV.wait(t, cv_e[(gj, ts - 2)])
        elif gj >= 1:
            b.cV.wait(t, cv_e[(gj - 1, ts + 6)])
        for ib in range(4):
            for hh in range(2):
                ins = t.matmul(
                    b.psum[:, 2048 + (ts % 2) * 1024 + hh * 512:
                           2048 + (ts % 2) * 1024 + (hh + 1) * 512],
                    hid[par][:, ib * 1024 + ts * 128: ib * 1024 + ts * 128 + 128],
                    w2c[par][:, ib * 1024 + hh * 512: ib * 1024 + (hh + 1) * 512],
                    start=(ib == 0), stop=(ib == 3),
                    skip_group_check=True)
        b.cT.inc(ins)

    @blk.scalar
    def _(sc):
        for gi in range(NGI):
            for ib in range(4):
                b.cT.wait(sc, ct_g[(gi, ib)])
                if gi * 4 + ib >= 2:
                    gj, jb = divmod(gi * 4 + ib - 2, 4)
                    b.cV.wait(sc, cv_m[(gj, jb)])
                b.cA.inc(sc.activation(sg[ib % 2], b.psum[:, 0:1024], AF.Silu))
                if ib == 0 and gi + 1 < NGI:
                    gn = gi + 1
                    _, rem_n = divmod(gn, E * NIC)
                    e_n, _ = divmod(rem_n, NIC)
                    b.qW.wait(sc, b.wdone[f"w3_{e_n}"])
                    emit_w13_load(b, sc, gn)
                if ib == 3 and gi + 1 < NGI:
                    gn = gi + 1
                    _, rem_n = divmod(gn, E * NIC)
                    e_n, _ = divmod(rem_n, NIC)
                    if gi >= 1:
                        b.cT.wait(sc, ct_eo[(gi - 1, 7)])
                    b.qW.wait(sc, b.wdone[f"w2_{e_n}"])
                    emit_w2_load(b, sc, gn)

    @blk.tensor
    def _(t):
        for gi in range(NGI + 1):
            tt = tt_of(gi) if gi < NGI else NTF - 1
            par = gi % 2
            for ib in range(4):
                if gi < NGI:
                    if ib == 0:
                        b.qA.wait(t, qa0 + max(48 * gi - 16, 0))
                    if gi * 4 + ib >= 1:
                        gj, jb = divmod(gi * 4 + ib - 1, 4)
                        b.cA.wait(t, ca_s(gj, jb))
                    for cb in range(DC):
                        for th in range(2):
                            ins = t.matmul(
                                b.psum[:, th * 512:(th + 1) * 512],
                                w13[par * 2][:, cb * IC + ib * 128: cb * IC + ib * 128 + 128],
                                xnfm[:, cb * T + tt * TTF + th * 512:
                                     cb * T + tt * TTF + (th + 1) * 512],
                                start=(cb == 0), stop=(cb == DC - 1),
                                skip_group_check=True)
                    b.cT.inc(ins)
                    if gi * 4 + ib >= 1:
                        gj, jb = divmod(gi * 4 + ib - 1, 4)
                        b.cV.wait(t, cv_m[(gj, jb)])
                    for cb in range(DC):
                        for th in range(2):
                            ins = t.matmul(
                                b.psum[:, 1024 + th * 512:1024 + (th + 1) * 512],
                                w13[par * 2 + 1][:, cb * IC + ib * 128: cb * IC + ib * 128 + 128],
                                xnfm[:, cb * T + tt * TTF + th * 512:
                                     cb * T + tt * TTF + (th + 1) * 512],
                                start=(cb == 0), stop=(cb == DC - 1),
                                skip_group_check=True)
                    b.cT.inc(ins)
                if gi >= 1:
                    emit_eo(t, gi - 1, 2 * ib)
                    emit_eo(t, gi - 1, 2 * ib + 1)

    @blk.vector
    def _(v):
        for gi in range(NGI + 1):
            par = gi % 2
            for ib in range(4):
                if gi < NGI:
                    b.cA.wait(v, ca_s(gi, ib))
                    b.cT.wait(v, ct_u[(gi, ib)])
                    b.cV.inc(v.tensor_tensor(hid[par][:, ib * 1024:(ib + 1) * 1024],
                                             sg[ib % 2], b.psum[:, 1024:2048],
                                             OP.mult))
                if gi >= 1:
                    gj = gi - 1
                    tt, rem = divmod(gj, E * NIC)
                    e, ic = divmod(rem, NIC)
                    for ts in (2 * ib, 2 * ib + 1):
                        b.cT.wait(v, ct_eo[(gj, ts)])
                        if e == 0 and ic == 0:
                            b.qS.wait(v, qs_ld(tt, ts))
                        pb = b.psum[:, 2048 + (ts % 2) * 1024:
                                    2048 + (ts % 2) * 1024 + 1024]
                        b.cV.inc(v.scalar_tensor_tensor(
                            acc[:, ts * 1024:(ts + 1) * 1024],
                            pb, b.tw_all[:, (tt * 8 + ts) * 8 + e: (tt * 8 + ts) * 8 + e + 1],
                            acc[:, ts * 1024:(ts + 1) * 1024],
                            OP.mult, OP.add))

    @blk.sync
    def _(s):
        for tt in range(NTF):
            if tt > 0:
                b.qS.wait(s, s0 + 256 * tt)
            for ts in range(8):
                b.qS.inc(s.dma_start(
                    out=acc[:, ts * 1024:(ts + 1) * 1024],
                    in_=x_old[(tt * 8 + ts) * 128:(tt * 8 + ts) * 128 + 128, :]), 16)
            gl = gi_of(tt, E - 1, NIC - 1)
            for ts in range(8):
                b.cV.wait(s, cv_e[(gl, ts)])
                b.qS.inc(s.dma_start(
                    out=x_new[(tt * 8 + ts) * 128:(tt * 8 + ts) * 128 + 128, :],
                    in_=acc[:, ts * 1024:(ts + 1) * 1024]), 16)

    assert b.cT.val == t0 + 16 * NGI
    assert b.cA.val == a0 + 4 * NGI
    assert b.cV.val == v0 + 12 * NGI
    assert b.qA.val == qa0 + 48 * (NGI - 1)
    assert b.qS.val == s0 + 256 * NTF


def emit_w13_load(b, sc, gi):
    _, rem = divmod(gi, E * NIC)
    e, ic = divmod(rem, NIC)
    par = gi % 2
    w13_0 = b.carve(96 * 1024 + (par * 2) * 8 * 1024, 128, DC * IC, BF16)
    w13_1 = b.carve(96 * 1024 + (par * 2 + 1) * 8 * 1024, 128, DC * IC, BF16)
    b.qA.inc(sc.dma_start(
        out=w13_0.rearrange("p (c i) -> p c i", c=DC),
        in_=b.w1bf[e][:, ic * IC:(ic + 1) * IC]
            .rearrange("(c p) i -> p c i", p=128)), 16)
    b.qA.inc(sc.dma_start(
        out=w13_1.rearrange("p (c i) -> p c i", c=DC),
        in_=b.w3bf[e][:, ic * IC:(ic + 1) * IC]
            .rearrange("(c p) i -> p c i", p=128)), 16)


def emit_w2_load(b, sc, gi):
    _, rem = divmod(gi, E * NIC)
    e, ic = divmod(rem, NIC)
    w2c = b.carve(128 * 1024 + (gi % 2) * 8 * 1024, 128, 4 * 1024, BF16)
    b.qA.inc(sc.dma_start(
        out=w2c.rearrange("p (i h) -> p i h", i=4),
        in_=b.w2bf[e][ic * IC:(ic + 1) * IC, :]
            .rearrange("(i p) h -> p i h", p=128)), 16)


def emit_ffn_prologue(b, blk):
    @blk.scalar
    def _(sc):
        b.qW.wait(sc, b.wdone["w3_0"])
        emit_w13_load(b, sc, 0)
        b.qW.wait(sc, b.wdone["w2_0"])
        emit_w2_load(b, sc, 0)


def emit_finish(b, blk, x_loc):
    if x_loc is not None:
        xt = b.carve(P2, 128, 2 * 1024, F32)

        @blk.sync
        def _(s):
            for m in range(NCH):
                buf = xt[:, (m % 2) * 1024:(m % 2) * 1024 + 1024]
                b.qS.inc(s.dma_start(out=buf, in_=x_loc[m * 128:(m + 1) * 128, :]), 16)
                b.qS.wait(s)
                b.qS.inc(s.dma_start(out=b.out[m * 128:(m + 1) * 128, :], in_=buf), 16)
                if m >= 1:
                    b.qS.wait(s, b.qS.val - 16)

    @blk.sync
    def _(s):
        b.qS.wait(s)
        b.qG.wait(s)
        b.qA.wait(s)

    @blk.gpsimd
    def _(g):
        b.qS.wait(g)
        b.qG.wait(g)
